# version 15
# baseline (speedup 1.0000x reference)
import numpy as np
import ml_dtypes
from contextlib import ExitStack

import concourse.bass as bass
import concourse.mybir as mybir
from concourse.bass_utils import run_bass_kernel_spmd

F32 = mybir.dt.float32
I32 = mybir.dt.int32
BF16 = mybir.dt.bfloat16
F16 = mybir.dt.float16
ALU = mybir.AluOpType
ACTF = mybir.ActivationFunctionType
AXX = mybir.AxisListType.X
BF = ml_dtypes.bfloat16

D = 1024
NM = 8
T = 4096
SW = 512
NSW = 8
NT = 32
NH = 16
I3 = 3072
NI = 24
IC = 1024
NIC = 3
CW = 1024
NCW = 4
QMAX = 32767.0
EPS = 1e-5


def build_nc(stop=None, debug=False):
    nc = bass.Bass("TRN2", target_bir_lowering=False)

    x0T_h = nc.dram_tensor("x0T", [D, T], F32, kind="ExternalInput").ap()
    wqT_h = nc.dram_tensor("wqT", [D, D], F32, kind="ExternalInput").ap()
    wkT_h = nc.dram_tensor("wkT", [D, D], F32, kind="ExternalInput").ap()
    wvT_h = nc.dram_tensor("wvT", [D, D], F32, kind="ExternalInput").ap()
    wo_pk_h = nc.dram_tensor("wo_pk", [64, NH * D], F32, kind="ExternalInput").ap()
    wupT_h = nc.dram_tensor("wupT", [D, I3], F32, kind="ExternalInput").ap()
    wgT_h = nc.dram_tensor("wgT", [D, I3], F32, kind="ExternalInput").ap()
    wdT_h = nc.dram_tensor("wdT", [I3, D], F32, kind="ExternalInput").ap()
    wup_n_h = nc.dram_tensor("wup_n", [I3, D], F32, kind="ExternalInput").ap()
    wg_n_h = nc.dram_tensor("wg_n", [I3, D], F32, kind="ExternalInput").ap()
    wd_n_h = nc.dram_tensor("wd_n", [D, I3], F32, kind="ExternalInput").ap()
    ones_b_h = nc.dram_tensor("ones_b", [128, 128], BF16, kind="ExternalInput").ap()
    ident_h = nc.dram_tensor("ident", [128, 128], F32, kind="ExternalInput").ap()
    ones_row_h = nc.dram_tensor("ones_row", [1, 128], F32, kind="ExternalInput").ap()
    y_h = nc.dram_tensor("y", [D, T], F32, kind="ExternalOutput").ap()
    if debug == "lite":
        dbg_x1 = nc.dram_tensor("dbg_x1", [D, T], F32, kind="ExternalOutput").ap()
        dbg_a = nc.dram_tensor("dbg_a", [128, NM * T], BF16, kind="ExternalOutput").ap()
        dbg_up = nc.dram_tensor("dbg_up", [I3, T], F16, kind="ExternalOutput").ap()
        dbg_g2 = nc.dram_tensor("dbg_g2", [I3, T], F16, kind="ExternalOutput").ap()
        dbg_o = nc.dram_tensor("dbg_o", [I3, T], F16, kind="ExternalOutput").ap()
    if debug is True:
        dbg_x1 = nc.dram_tensor("dbg_x1", [D, T], F32, kind="ExternalOutput").ap()
        dbg_up = nc.dram_tensor("dbg_up", [I3, T], F16, kind="ExternalOutput").ap()
        dbg_g = nc.dram_tensor("dbg_g", [I3, T], F16, kind="ExternalOutput").ap()
        dbg_g2 = nc.dram_tensor("dbg_g2", [I3, T], F16, kind="ExternalOutput").ap()
        dbg_o = nc.dram_tensor("dbg_o", [I3, T], F16, kind="ExternalOutput").ap()
        dbg_wd = nc.dram_tensor("dbg_wd", [128, NI * D], BF16, kind="ExternalOutput").ap()
        dbg_scal = nc.dram_tensor("dbg_scal", [1, 32], F32, kind="ExternalOutput").ap()
        dbg_cols = nc.dram_tensor("dbg_cols", [128, 8], F32, kind="ExternalOutput").ap()
        dbg_m = nc.dram_tensor("dbg_m", [128, 8192], BF16, kind="ExternalOutput").ap()
        dbg_tu = nc.dram_tensor("dbg_tu", [32, I3], F32, kind="ExternalOutput").ap()
        dbg_iu = nc.dram_tensor("dbg_iu", [32, I3], F32, kind="ExternalOutput").ap()
        dbg_tg = nc.dram_tensor("dbg_tg", [32, I3], F32, kind="ExternalOutput").ap()
        dbg_a = nc.dram_tensor("dbg_a", [128, NM * T], BF16, kind="ExternalOutput").ap()
        dbg_vk = nc.dram_tensor("dbg_vk", [64, 1024], BF16, kind="ExternalOutput").ap()
        dbg_m2 = nc.dram_tensor("dbg_m2", [128, 8192], BF16, kind="ExternalOutput").ap()

    sp_x1 = nc.dram_tensor("sp_x1", [D, T], F32, kind="Internal").ap()
    sp_up = nc.dram_tensor("sp_up", [I3, T], F16, kind="Internal").ap()
    sp_g = nc.dram_tensor("sp_g", [I3, T], F16, kind="Internal").ap()
    sp_g2 = nc.dram_tensor("sp_g2", [I3, T], F16, kind="Internal").ap()
    sp_o = nc.dram_tensor("sp_o", [I3, T], F16, kind="Internal").ap()
    sp_wd = nc.dram_tensor("sp_wd", [128, NI * D], BF16, kind="Internal").ap()
    dT_up = nc.dram_tensor("dT_up", [32, I3], F32, kind="Internal").ap()
    dI_up = nc.dram_tensor("dI_up", [32, I3], F32, kind="Internal").ap()
    dT_g = nc.dram_tensor("dT_g", [32, I3], F32, kind="Internal").ap()
    dI_g = nc.dram_tensor("dI_g", [32, I3], F32, kind="Internal").ap()
    dT_d = nc.dram_tensor("dT_d", [96, D], F32, kind="Internal").ap()
    dI_d = nc.dram_tensor("dI_d", [96, D], F32, kind="Internal").ap()
    tmpcol_t = nc.dram_tensor("tmpcol", [128, 1], F32, kind="Internal")
    tmpcol = tmpcol_t.ap()
    tmpcol_row = tmpcol_t.ap().flatten().unsqueeze(0)
    cckv_i = nc.dram_tensor("cckv_i", [64, 1024], F32, kind="Internal").ap()
    cckv_o = nc.dram_tensor("cckv_o", [64, 1024], F32, kind="Internal").ap()
    ccs_i = [nc.dram_tensor(f"ccs_i{k}", [1, 2], F32, kind="Internal").ap() for k in range(4)]
    ccs_o = [nc.dram_tensor(f"ccs_o{k}", [1, 2], F32, kind="Internal").ap() for k in range(4)]

    es = ExitStack()
    ent = es.enter_context

    A = ent(nc.sbuf_tensor("A", [128, NM * T], BF16))
    wb1 = ent(nc.sbuf_tensor("wb1", [128, 8192], BF16))
    wb2 = ent(nc.sbuf_tensor("wb2", [128, 8192], BF16))
    obuf = ent(nc.sbuf_tensor("obuf", [128, 8192], BF16))
    wstat = ent(nc.sbuf_tensor("wstat", [128, 6144], F32))
    cst = ent(nc.sbuf_tensor("cst", [128, 8192], F16))
    k_sb = ent(nc.sbuf_tensor("k_sb", [128, 1024], BF16))
    v_sb = ent(nc.sbuf_tensor("v_sb", [128, 1024], BF16))
    vk_b = ent(nc.sbuf_tensor("vk_b", [64, 1024], BF16))
    sqt = ent(nc.sbuf_tensor("sqt", [128, 1024], BF16))
    mtile = ent(nc.sbuf_tensor("mtile", [128, 512], F32))
    ttile = ent(nc.sbuf_tensor("ttile", [128, 512], F32))
    rtile = ent(nc.sbuf_tensor("rtile", [128, 512], F32))
    mrtile = ent(nc.sbuf_tensor("mrtile", [128, 512], F32))
    itmp = ent(nc.sbuf_tensor("itmp", [128, 2048], I32))
    itmpd = ent(nc.sbuf_tensor("itmpd", [128, 1024], I32))
    ftmp = ent(nc.sbuf_tensor("ftmp", [128, 1024], F32))
    spst = ent(nc.sbuf_tensor("spst", [128, 2048], BF16))
    rost = ent(nc.sbuf_tensor("rost", [128, 1024], BF16))
    wdst = ent(nc.sbuf_tensor("wdst", [128, 2048], BF16))
    strip = ent(nc.sbuf_tensor("strip", [96, 256], F32))
    stt_s = ent(nc.sbuf_tensor("stt_s", [128, 128], F32))
    colstrip = ent(nc.sbuf_tensor("colstrip", [128, 512], F32))
    rmcol = ent(nc.sbuf_tensor("rmcol", [128, 8], F32))
    cols = ent(nc.sbuf_tensor("cols", [128, 8], F32))
    scal = ent(nc.sbuf_tensor("scal", [1, 32], F32))
    rowb = ent(nc.sbuf_tensor("rowb", [1, 128], F32))
    ones_b = ent(nc.sbuf_tensor("onesb", [128, 128], BF16))
    ident = ent(nc.sbuf_tensor("identt", [128, 128], F32))
    ones_row = ent(nc.sbuf_tensor("onesr", [1, 128], F32))
    ccol = ent(nc.sbuf_tensor("ccol", [128, 2], F32))
    scal2 = ent(nc.sbuf_tensor("scal2", [1, 32], F32))
    stt_i = ent(nc.sbuf_tensor("stt_i", [128, 128], F32))
    x1st = ent(nc.sbuf_tensor("x1st", [128, 1024], F32))
    wstg = x1st

    ps = [ent(nc.psum_tensor(f"ps{i}", [128, 512], F32)) for i in range(7)]
    psT = ent(nc.psum_tensor("psT", [128, 512], F32))

    semnames = ["ldx", "lst", "lq0", "lq1", "lm", "wr", "tv", "vv", "aa", "gg"]
    sems = {n: ent(nc.semaphore(n)) for n in semnames}
    C = {n: 0 for n in semnames}
    streams = {k: [] for k in ("sync", "vector", "scalar", "tensor", "gpsimd")}

    def em(eng, f):
        streams[eng].append(f)

    def W(eng, sem, val=None):
        v = C[sem] if val is None else val
        assert v <= C[sem], (eng, sem, v, C[sem])
        if v > 0:
            em(eng, lambda e, s=sems[sem], v=v: e.wait_ge(s, v))

    def dma(eng, sem, out, in_):
        C[sem] += 16
        em(eng, lambda e, o=out, i=in_, s=sems[sem]: e.dma_start(out=o, in_=i).then_inc(s, 16))
        return C[sem]

    def op(eng, f, sem=None):
        if sem is None:
            em(eng, lambda e, f=f: f(e))
            return None
        C[sem] += 1
        em(eng, lambda e, f=f, s=sems[sem]: f(e).then_inc(s, 1))
        return C[sem]

    LQ = ["lq0", "lq1"]

    def vch(f):
        v = op("vector", f, "vv")
        W("vector", "vv", v)
        return v

    class _Stop(Exception):
        pass

    def ckpt(name):
        if stop == name:
            raise _Stop()

    try:
        op("vector", lambda e: e.memset(ccol[:, 0:1], 65536.0))
        dma("sync", "ldx", scal[0:1, 0:32], ones_row_h[0:1, 0:32])
        dma("sync", "ldx", scal2[0:1, 0:32], ones_row_h[0:1, 0:32])
        dma("sync", "ldx", ones_b[:], ones_b_h[:, :])
        dma("sync", "ldx", ident[:], ident_h[:, :])
        dma("sync", "ldx", ones_row[:], ones_row_h[:, :])

        cast_v = 0
        for mc in range(NM):
            for hh in range(2):
                slot = ((mc * 2 + hh) % 2) * 2048
                if mc * 2 + hh >= 2:
                    W("sync", "vv", cast_v - 1)
                dl = dma("sync", "ldx", wstat[:, slot:slot + 2048],
                         x0T_h[mc * 128:(mc + 1) * 128, hh * 2048:(hh + 1) * 2048])
                W("vector", "ldx", dl)
                cast_v = op("vector", lambda e, mc=mc, hh=hh, slot=slot: e.tensor_copy(
                    A[:, mc * T + hh * 2048: mc * T + (hh + 1) * 2048],
                    wstat[:, slot:slot + 2048]), "vv")
        for (w_h, dst) in ((wkT_h, wb1), (wvT_h, wb2)):
            for mc in range(NM):
                slot = (mc % 2) * 2048
                W("sync", "vv", cast_v - 1)
                dl = dma("sync", "ldx", wstat[:, slot:slot + 1024],
                         w_h[mc * 128:(mc + 1) * 128, :])
                W("vector", "ldx", dl)
                cast_v = op("vector", lambda e, mc=mc, dst=dst, slot=slot: e.tensor_copy(
                    dst[:, mc * 1024:(mc + 1) * 1024], wstat[:, slot:slot + 1024]), "vv")
        p0_cast_done = cast_v

        ckpt('P0')
        W("tensor", "vv", p0_cast_done)
        aev = 0
        kv_last = 0
        tvp = 0
        for t in range(NT):
            if t > 0:
                W("tensor", "aa", aev)
            for mc in range(NM):
                lhs = A[:, mc * T + t * 128: mc * T + t * 128 + 128]
                for s2 in range(2):
                    op("tensor", lambda e, lhs=lhs, mc=mc, s2=s2: e.matmul(
                        ps[s2][:, :], lhs,
                        wb1[:, mc * 1024 + s2 * 512: mc * 1024 + (s2 + 1) * 512],
                        start=(mc == 0), stop=(mc == NM - 1)))
                    f2 = (lambda e, lhs=lhs, mc=mc, s2=s2: e.matmul(
                        ps[2 + s2][:, :], lhs,
                        wb2[:, mc * 1024 + s2 * 512: mc * 1024 + (s2 + 1) * 512],
                        start=(mc == 0), stop=(mc == NM - 1)))
                    if mc == NM - 1 and s2 == 1:
                        tvp = op("tensor", f2, "tv")
                    else:
                        op("tensor", f2)
            W("scalar", "tv", tvp)
            if t > 0:
                W("scalar", "tv", kv_last)
            op("scalar", lambda e: e.activation(k_sb[:, 0:512], ps[0][:, :], ACTF.Copy, bias=0.0, scale=1.0))
            op("scalar", lambda e: e.activation(k_sb[:, 512:1024], ps[1][:, :], ACTF.Copy, bias=0.0, scale=1.0))
            op("scalar", lambda e: e.activation(v_sb[:, 0:512], ps[2][:, :], ACTF.Copy, bias=0.0, scale=1.0))
            aev = op("scalar", lambda e: e.activation(v_sb[:, 512:1024], ps[3][:, :], ACTF.Copy, bias=0.0, scale=1.0), "aa")
            W("tensor", "aa", aev)
            for h in range(NH):
                bank = 4 + h // 8
                c0 = (h % 8) * 64
                f = (lambda e, h=h, bank=bank, c0=c0, t=t: e.matmul(
                    ps[bank][0:64, c0:c0 + 64], v_sb[:, h * 64:(h + 1) * 64],
                    k_sb[:, h * 64:(h + 1) * 64],
                    start=(t == 0 and (h % 8) == 0), stop=(t == NT - 1), skip_group_check=True))
                if h == NH - 1:
                    kv_last = op("tensor", f, "tv")
                else:
                    op("tensor", f)

        ckpt('P1')
        prev_stat_v = p0_cast_done
        prev_tr_t = 0

        def emit_stat_tile(dl, slot, tw, nb, s_off, i_off):
            W("vector", "lst", dl)
            if prev_tr_t:
                W("vector", "tv", prev_tr_t)
            vch(lambda e, slot=slot, tw=tw, nb=nb: e.reduce_max(
                stt_s[:, 0:nb], wstat[:, slot:slot + tw].rearrange("p (b u) -> p b u", u=32),
                axis=AXX, apply_absolute_value=True))
            vch(lambda e, nb=nb, s_off=s_off: e.tensor_scalar(
                stt_s[:, s_off:s_off + nb], stt_s[:, 0:nb], 1.0 / 7.0, 1e-12, ALU.mult, ALU.max))
            return vch(lambda e, nb=nb, s_off=s_off, i_off=i_off: e.reciprocal(
                stt_i[:, i_off:i_off + nb], stt_s[:, s_off:s_off + nb]))

        prev_strip_dma = [0]

        def emit_stat_store(vguard, nb, s_off, i_off, dTd, dId, r0, c0):
            W("tensor", "vv", vguard)
            op("tensor", lambda e, nb=nb, s_off=s_off: e.transpose(
                psT[0:nb, 0:128], stt_s[:, s_off:s_off + nb], ident[:]))
            t1 = op("tensor", lambda e, nb=nb, i_off=i_off: e.transpose(
                psT[0:nb, 128:256], stt_i[:, i_off:i_off + nb], ident[:]), "tv")
            W("vector", "tv", t1)
            if prev_strip_dma[0]:
                W("vector", "wr", prev_strip_dma[0])
            op("vector", lambda e, nb=nb: e.tensor_copy(strip[0:nb, 0:128], psT[0:nb, 0:128]))
            vg = op("vector", lambda e, nb=nb: e.tensor_copy(strip[0:nb, 128:256], psT[0:nb, 128:256]), "vv")
            W("gpsimd", "vv", vg)
            dma("gpsimd", "wr", dTd[r0:r0 + nb, c0:c0 + 128], strip[0:nb, 0:128])
            prev_strip_dma[0] = dma("gpsimd", "wr", dId[r0:r0 + nb, c0:c0 + 128], strip[0:nb, 128:256])
            return t1

        for (wh, npt, tw, nb, dTd, dId) in ((wup_n_h, NI, 1024, 32, dT_up, dI_up),
                                            (wg_n_h, NI, 1024, 32, dT_g, dI_g)):
            for pt in range(npt):
                slot = (pt % 2) * 1536
                W("sync", "vv", prev_stat_v)
                dl = dma("sync", "lst", wstat[:, slot:slot + tw], wh[pt * 128:(pt + 1) * 128, 0:tw])
                prev_stat_v = emit_stat_tile(dl, slot, tw, nb, 32, 64)
                prev_tr_t = emit_stat_store(prev_stat_v, nb, 32, 64, dTd, dId, 0, pt * 128)
        for pt in range(8):
            for hh in range(2):
                slot = (hh % 2) * 1536
                W("sync", "vv", prev_stat_v)
                dl = dma("sync", "lst", wstat[:, slot:slot + 1536],
                         wd_n_h[pt * 128:(pt + 1) * 128, hh * 1536:(hh + 1) * 1536])
                prev_stat_v = emit_stat_tile(dl, slot, 1536, 48, 48, 0)
                prev_tr_t = emit_stat_store(prev_stat_v, 48, 48, 0, dT_d, dI_d, hh * 48, pt * 128)
        stats_tables_done = C["wr"]

        ckpt('stats')
        W("vector", "tv", kv_last)
        op("vector", lambda e: e.tensor_copy(ftmp[0:64, 0:512], ps[4][0:64, :]))
        v1 = op("vector", lambda e: e.tensor_copy(ftmp[0:64, 512:1024], ps[5][0:64, :]), "vv")
        ckpt("P2kv1")
        W("gpsimd", "vv", v1)
        dkv = dma("gpsimd", "wr", cckv_i[:, :], ftmp[0:64, :])
        ckpt("P2kv2")
        W("gpsimd", "wr", dkv)
        cc_kv = op("gpsimd", lambda e: e.collective_compute(
            "AllReduce", ALU.add, replica_groups=[[0, 1], [2, 3], [4, 5], [6, 7]],
            ins=[cckv_i[:, :]], outs=[cckv_o[:, :]]), "gg")
        ckpt("P2kv3")
        W("sync", "gg", cc_kv)
        dvk = dma("sync", "lm", ftmp[0:64, :], cckv_o[:, :])
        W("vector", "lm", dvk)
        vkc = op("vector", lambda e: e.tensor_copy(vk_b[:, :], ftmp[0:64, :]), "vv")
        ckpt("P2a")

        wq_done = 0
        for mc in range(NM):
            if wq_done:
                W("sync", "vv", wq_done)
            dl = dma("sync", "ldx", wstg[:, 0:1024], wqT_h[mc * 128:(mc + 1) * 128, :])
            W("vector", "ldx", dl)
            if mc == 0:
                W("vector", "tv", kv_last)
            wq_done = op("vector", lambda e, mc=mc: e.tensor_copy(
                wb1[:, mc * 1024:(mc + 1) * 1024], wstg[:, 0:1024]), "vv")

        ckpt("P2b")
        m_done = 0
        woc = wq_done
        modd_dma = [0, 0]
        for half in range(2):
            for hh in range(8):
                W("sync", "vv", woc)
                dl = dma("sync", "ldx", wstg[0:64, 0:1024],
                         wo_pk_h[0:64, (half * 8 + hh) * 1024:(half * 8 + hh + 1) * 1024])
                W("vector", "ldx", dl)
                woc = op("vector", lambda e, hh=hh: e.tensor_copy(
                    obuf[0:64, hh * 1024:(hh + 1) * 1024], wstg[0:64, 0:1024]), "vv")
            W("tensor", "vv", max(woc, vkc))
            for hh in range(8):
                h = half * 8 + hh
                j, beta = h // 2, h % 2
                b0 = (hh % 2) * 2
                if m_done:
                    W("tensor", "vv", m_done)
                mm = 0
                for s2 in range(2):
                    mm = op("tensor", lambda e, h=h, hh=hh, s2=s2, b0=b0: e.matmul(
                        ps[b0 + s2][0:64, :],
                        vk_b[0:64, h * 64:(h + 1) * 64],
                        obuf[0:64, hh * 1024 + s2 * 512: hh * 1024 + (s2 + 1) * 512],
                        start=True, stop=True), "tv" if s2 == 1 else None)
                W("vector", "tv", mm)
                if beta == 0:
                    for s2 in range(2):
                        m_done = op("vector", lambda e, j=j, s2=s2, b0=b0: e.tensor_scalar(
                            wb2[0:64, j * 1024 + s2 * 512: j * 1024 + (s2 + 1) * 512],
                            ps[b0 + s2][0:64, :], 0.125, None, ALU.mult), "vv")
                else:
                    for s2 in range(2):
                        if modd_dma[s2]:
                            W("vector", "wr", modd_dma[s2])
                        m_done = op("vector", lambda e, s2=s2, b0=b0: e.tensor_scalar(
                            spst[0:64, s2 * 512:(s2 + 1) * 512],
                            ps[b0 + s2][0:64, :], 0.125, None, ALU.mult), "vv")
                        W("gpsimd", "vv", m_done)
                        modd_dma[s2] = dma("gpsimd", "wr",
                                           wb2[64:128, j * 1024 + s2 * 512: j * 1024 + (s2 + 1) * 512],
                                           spst[0:64, s2 * 512:(s2 + 1) * 512])
        m_wr_done = C["wr"]

        W("tensor", "vv", m_done)
        W("tensor", "wr", m_wr_done)
        aq = {}
        vxr = {}
        mst_oc = {}
        x1_tile = 0
        x1_slot_dma = [0, 0]
        stats_read_done = 0
        for sw in range(NSW):
            for oc in range(8):
                bank = oc % 2
                if oc >= 2:
                    W("tensor", "aa", aq[oc - 2])
                elif sw > 0:
                    W("tensor", "aa", aq[6 + oc])
                for mc in range(NM):
                    op("tensor", lambda e, oc=oc, mc=mc, sw=sw, bank=bank: e.matmul(
                        ps[bank][:, :],
                        wb1[:, mc * 1024 + oc * 128: mc * 1024 + oc * 128 + 128],
                        A[:, mc * T + sw * SW: mc * T + (sw + 1) * SW],
                        start=(mc == 0), stop=(mc == NM - 1)),
                        "tv" if mc == NM - 1 else None)
                mmq = C["tv"]
                W("scalar", "tv", mmq)
                aq[oc] = op("scalar", lambda e, oc=oc, bank=bank: e.activation(
                    obuf[:, oc * 512:(oc + 1) * 512], ps[bank][:, :],
                    ACTF.Copy, bias=0.0, scale=1.0), "aa")
            W("tensor", "aa", aq[7])
            for oc in range(8):
                bank = 2 + oc % 2
                if oc >= 2:
                    W("tensor", "vv", vxr[oc - 2])
                elif sw > 0:
                    W("tensor", "vv", max(vxr[6 + oc], stats_read_done))
                for tch in range(8):
                    op("tensor", lambda e, oc=oc, tch=tch, bank=bank: e.matmul(
                        ps[bank][:, :],
                        wb2[:, tch * 1024 + oc * 128: tch * 1024 + oc * 128 + 128],
                        obuf[:, tch * 512:(tch + 1) * 512],
                        start=(tch == 0), stop=(tch == 7)),
                        "tv" if tch == 7 else None)
                mmo = C["tv"]
                W("vector", "tv", mmo)
                if oc >= 2:
                    W("vector", "tv", mst_oc[oc - 2])
                op("vector", lambda e, oc=oc, sw=sw, bank=bank: e.tensor_tensor(
                    wstat[:, oc * 512:(oc + 1) * 512],
                    ps[bank][:, :],
                    A[:, oc * T + sw * SW: oc * T + (sw + 1) * SW], ALU.add))
                op("vector", lambda e, oc=oc: e.tensor_copy(
                    obuf[:, 4096 + oc * 512: 4096 + (oc + 1) * 512],
                    wstat[:, oc * 512:(oc + 1) * 512]))
                sqo = (oc % 2) * 512
                vxr[oc] = op("vector", lambda e, oc=oc, sqo=sqo: e.tensor_tensor(
                    sqt[:, sqo:sqo + 512],
                    obuf[:, 4096 + oc * 512: 4096 + (oc + 1) * 512],
                    obuf[:, 4096 + oc * 512: 4096 + (oc + 1) * 512], ALU.mult), "vv")
                W("tensor", "vv", vxr[oc])
                op("tensor", lambda e, oc=oc, sw=sw: e.matmul(
                    ps[4][:, :], ones_b[:, :],
                    obuf[:, 4096 + oc * 512: 4096 + (oc + 1) * 512],
                    start=(oc == 0), stop=(oc == 7), skip_group_check=True))
                mst_oc[oc] = op("tensor", lambda e, oc=oc, sqo=sqo: e.matmul(
                    ps[5][:, :], ones_b[:, :], sqt[:, sqo:sqo + 512],
                    start=(oc == 0), stop=(oc == 7), skip_group_check=True), "tv")
            W("vector", "tv", mst_oc[7])
            op("vector", lambda e: e.tensor_scalar(mtile[:, :], ps[4][:, :], 1.0 / 1024.0, None, ALU.mult))
            op("vector", lambda e: e.tensor_tensor(ttile[:, :], mtile[:, :], mtile[:, :], ALU.mult))
            op("vector", lambda e: e.tensor_scalar(
                rtile[:, :], ps[5][:, :], 1.0 / 1024.0, EPS, ALU.mult, ALU.add))
            va = op("vector", lambda e: e.tensor_tensor(
                ttile[:, :], rtile[:, :], ttile[:, :], ALU.subtract), "vv")
            stats_read_done = va
            W("scalar", "vv", va)
            sqv = op("scalar", lambda e: e.activation(rtile[:, :], ttile[:, :], ACTF.Sqrt,
                                                      bias=0.0, scale=1.0), "aa")
            W("vector", "aa", sqv)
            op("vector", lambda e: e.reciprocal(ttile[:, :], rtile[:, :]))
            op("vector", lambda e: e.tensor_tensor(mrtile[:, :], mtile[:, :], ttile[:, :], ALU.mult))
            for oc in range(8):
                slot = x1_tile % 2
                op("vector", lambda e, oc=oc: e.tensor_tensor(
                    ftmp[:, 0:512],
                    wstat[:, oc * 512:(oc + 1) * 512], ttile[:, :], ALU.mult))
                if x1_slot_dma[slot]:
                    W("vector", "wr", x1_slot_dma[slot])
                op("vector", lambda e, slot=slot: e.tensor_tensor(
                    x1st[:, slot * 512:(slot + 1) * 512], ftmp[:, 0:512], mrtile[:, :], ALU.subtract))
                vap = op("vector", lambda e, slot=slot, n=x1_tile: e.reduce_max(
                    colstrip[:, n:n + 1], x1st[:, slot * 512:(slot + 1) * 512],
                    axis=AXX, apply_absolute_value=True), "vv")
                W("gpsimd", "vv", vap)
                x1_slot_dma[slot] = dma("gpsimd", "wr",
                                        sp_x1[oc * 128:(oc + 1) * 128, sw * SW:(sw + 1) * SW],
                                        x1st[:, slot * 512:(slot + 1) * 512])
                x1_tile += 1

        ckpt('P3')
        def scale_sync(colranges, cc_i, cc_o, derive, bcols):
            vred = 0
            for i, (s0, cnt) in enumerate(colranges):
                vred = op("vector", lambda e, s0=s0, cnt=cnt, i=i: e.reduce_max(
                    rmcol[:, i:i + 1], colstrip[:, s0:s0 + cnt], axis=AXX), "vv")
            prev_row_rd = 0
            for i in range(len(colranges)):
                W("gpsimd", "vv", vred)
                if prev_row_rd:
                    W("gpsimd", "lm", prev_row_rd)
                dcw = dma("gpsimd", "wr", tmpcol[:, :], rmcol[:, i:i + 1])
                W("sync", "wr", dcw)
                if i > 0:
                    W("sync", "vv", vred)
                prev_row_rd = dma("sync", "lm", rowb[:, :], tmpcol_row)
                W("vector", "lm", prev_row_rd)
                vred = op("vector", lambda e, i=i: e.reduce_max(
                    scal[0:1, 8 + i:9 + i], rowb[0:1, :], axis=AXX), "vv")
            k = len(colranges)
            W("gpsimd", "vv", vred)
            dci = dma("gpsimd", "wr", cc_i[0:1, 0:k], scal[0:1, 8:8 + k])
            W("gpsimd", "wr", dci)
            ccv = op("gpsimd", lambda e, cc_i=cc_i, cc_o=cc_o, k=k: e.collective_compute(
                "AllReduce", ALU.max, replica_groups=[[0, 1, 2, 3, 4, 5, 6, 7]],
                ins=[cc_i[0:1, 0:k]], outs=[cc_o[0:1, 0:k]]), "gg")
            W("sync", "gg", ccv)
            dro = dma("sync", "lm", scal[0:1, 12:12 + k], cc_o[0:1, 0:k])
            W("vector", "lm", dro)
            vd = derive()
            nb = len(bcols)
            b0 = bcols[0]
            assert bcols == list(range(b0, b0 + nb))
            W("tensor", "vv", vd)
            tb = op("tensor", lambda e, b0=b0, nb=nb: e.matmul(
                psT[:, 256:256 + nb], ones_row[0:1, :], scal[0:1, b0:b0 + nb],
                start=True, stop=True, skip_group_check=True), "tv")
            W("vector", "tv", tb)
            return op("vector", lambda e, nb=nb: e.tensor_copy(cols[:, 0:nb], psT[:, 256:256 + nb]), "vv")

        def derive1():
            vch(lambda e: e.tensor_scalar(
                scal[0:1, 16:17], scal[0:1, 12:13], 1.0 / QMAX, 1e-12, ALU.mult, ALU.max))
            vch(lambda e: e.reciprocal(scal2[0:1, 20:22], scal[0:1, 16:18]))
            return vch(lambda e: e.tensor_copy(scal[0:1, 17:18], scal2[0:1, 20:21]))
        sync1_done = scale_sync([(0, 64)], ccs_i[0], ccs_o[0], derive1, [16, 17])

        ckpt('sync1')
        W("sync", "wr")
        W("sync", "vv")
        if debug is True:
            m2dump = dma("gpsimd", "wr", dbg_m2[:, :], wb2[:, :])
            W("vector", "wr", m2dump)
        x1q_done = sync1_done
        rd_dma = [0, 0]
        for mc in range(NM):
            for cw in range(NCW):
                pslot = (mc * NCW + cw) % 2
                if rd_dma[pslot]:
                    W("sync", "vv", x1q_done)
                rd_dma[pslot] = dma("sync", LQ[pslot], wstat[:, pslot * 1024:(pslot + 1) * 1024],
                                    sp_x1[mc * 128:(mc + 1) * 128, cw * 1024:(cw + 1) * 1024])
                W("vector", LQ[pslot], rd_dma[pslot])
                op("vector", lambda e, pslot=pslot: e.tensor_scalar(
                    itmp[:, 0:1024], wstat[:, pslot * 1024:(pslot + 1) * 1024],
                    cols[:, 1:2], None, ALU.mult))
                x1q_done = op("vector", lambda e, mc=mc, cw=cw: e.tensor_scalar(
                    A[:, mc * T + cw * 1024: mc * T + (cw + 1) * 1024],
                    itmp[:, 0:1024], cols[:, 0:1], None, ALU.mult), "vv")

        spill_slot_dma = [0, 0, 0, 0]
        spill_slot_red = [0, 0, 0, 0]
        W("sync", "wr", stats_tables_done)
        W("sync", "vv")
        sexp_prev_use = [0, 0]
        mm_last_read_wb = 0
        utile = 0
        gtile = 0
        evac_mi = {0: [], 1: []}
        for ic in range(NIC):
            for mi, (wT_h, dT, dI, wbuf, spd) in enumerate(
                    ((wupT_h, dT_up, dI_up, wb1, sp_up), (wgT_h, dT_g, dI_g, wb2, sp_g))):
                for pc in range(NM):
                    pp = pc % 2
                    base = pp * 3072
                    if sexp_prev_use[pp]:
                        W("sync", "vv", sexp_prev_use[pp])
                    for b in range(4):
                        dma("sync", LQ[pp], wstat[32 * b:32 * (b + 1), base:base + 1024],
                            dI[4 * pc + b:4 * pc + b + 1, ic * IC:(ic + 1) * IC].broadcast_to([32, IC]))
                    for b in range(4):
                        dma("sync", LQ[pp], wstat[32 * b:32 * (b + 1), base + 1024:base + 2048],
                            dT[4 * pc + b:4 * pc + b + 1, ic * IC:(ic + 1) * IC].broadcast_to([32, IC]))
                    dl = dma("sync", LQ[pp], wstat[:, base + 2048:base + 3072],
                             wT_h[pc * 128:(pc + 1) * 128, ic * IC:(ic + 1) * IC])
                    W("vector", LQ[pp], dl)
                    if mm_last_read_wb:
                        W("vector", "tv", mm_last_read_wb)
                    op("vector", lambda e, base=base: e.tensor_tensor(
                        itmpd[:, 0:1024], wstat[:, base + 2048:base + 3072],
                        wstat[:, base:base + 1024], ALU.mult))
                    sexp_prev_use[pp] = op("vector", lambda e, base=base, pc=pc, wbuf=wbuf: e.tensor_tensor(
                        wbuf[:, pc * 1024:(pc + 1) * 1024], itmpd[:, 0:1024],
                        wstat[:, base + 1024:base + 2048], ALU.mult), "vv")
                deq_done = sexp_prev_use[(NM - 1) % 2]
                W("tensor", "vv", max(deq_done, x1q_done))
                pbank = [0, 1] if mi == 0 else [2, 3]
                evac_hist = evac_mi[mi]
                for j in range(8):
                    for sw in range(NSW):
                        kk = j * NSW + sw
                        bank = pbank[kk % 2]
                        if len(evac_hist) >= 2:
                            W("tensor", "aa", evac_hist[-2])
                        for mc in range(NM):
                            op("tensor", lambda e, j=j, sw=sw, mc=mc, bank=bank, wbuf=wbuf: e.matmul(
                                ps[bank][:, :],
                                wbuf[:, mc * 1024 + j * 128: mc * 1024 + j * 128 + 128],
                                A[:, mc * T + sw * SW: mc * T + (sw + 1) * SW],
                                start=(mc == 0), stop=(mc == NM - 1)),
                                "tv" if mc == NM - 1 else None)
                        mml = C["tv"]
                        if kk == 63:
                            mm_last_read_wb = mml
                        sidx = (2 * mi) + (kk % 2)
                        W("scalar", "tv", mml)
                        if spill_slot_dma[sidx]:
                            W("scalar", "wr", spill_slot_dma[sidx])
                            W("scalar", "vv", spill_slot_red[sidx])
                        ae = op("scalar", lambda e, sidx=sidx, bank=bank: e.activation(
                            cst[:, sidx * 512:(sidx + 1) * 512], ps[bank][:, :],
                            ACTF.Copy, bias=0.0, scale=1.0), "aa")
                        evac_hist.append(ae)
                        W("vector", "aa", ae)
                        n = utile if mi == 0 else (192 + gtile)
                        spill_slot_red[sidx] = op("vector", lambda e, sidx=sidx, n=n: e.reduce_max(
                            colstrip[:, n:n + 1], cst[:, sidx * 512:(sidx + 1) * 512],
                            axis=AXX, apply_absolute_value=True), "vv")
                        W("gpsimd", "vv", spill_slot_red[sidx])
                        r0 = ic * IC + j * 128
                        spill_slot_dma[sidx] = dma("gpsimd", "wr",
                                                   spd[r0:r0 + 128, sw * SW:(sw + 1) * SW],
                                                   cst[:, sidx * 512:(sidx + 1) * 512])
                        if mi == 0:
                            utile += 1
                        else:
                            gtile += 1
        p5_spills_done = C["wr"]

        ckpt('P5')
        def derive2():
            vch(lambda e: e.tensor_scalar(
                scal[0:1, 18:19], scal[0:1, 12:13], 1.0 / QMAX, 1e-12, ALU.mult, ALU.max))
            vch(lambda e: e.tensor_scalar(
                scal[0:1, 19:20], scal[0:1, 13:14], 1.0 / QMAX, 1e-12, ALU.mult, ALU.max))
            vch(lambda e: e.reciprocal(scal2[0:1, 22:24], scal[0:1, 18:20]))
            vch(lambda e: e.tensor_copy(scal[0:1, 22:24], scal2[0:1, 22:24]))
            vch(lambda e: e.tensor_scalar(
                scal[0:1, 20:21], scal[0:1, 19:20], 2.0, None, ALU.mult))
            vch(lambda e: e.tensor_copy(scal[0:1, 24:25], scal[0:1, 19:20]))
            vch(lambda e: e.tensor_copy(scal[0:1, 25:26], scal[0:1, 23:24]))
            return vch(lambda e: e.tensor_copy(scal[0:1, 26:27], scal[0:1, 22:23]))
        sync2_done = scale_sync([(0, 192), (192, 192)], ccs_i[1], ccs_o[1], derive2, [24, 25, 26])

        ckpt('sync2')
        W("sync", "wr")
        g2_slot_dma = [0, 0]
        g2_slot_red = [0, 0]
        gi_dma = [0, 0]
        gi_consumed = [0, 0]
        wd_stage_dma = [p5_spills_done, p5_spills_done]
        c1_tile = 0
        W("scalar", "vv", sync2_done)
        prev_g2_tt = sync2_done
        for ipt in range(NI):
            pp = ipt % 2
            base = pp * 3072
            W("sync", "wr")
            W("sync", "vv", sexp_prev_use[pp])
            for b in range(4):
                dma("sync", LQ[pp], wstat[32 * b:32 * (b + 1), base:base + 1024],
                    dI_d[4 * ipt + b:4 * ipt + b + 1, 0:D].broadcast_to([32, D]))
            for b in range(4):
                dma("sync", LQ[pp], wstat[32 * b:32 * (b + 1), base + 1024:base + 2048],
                    dT_d[4 * ipt + b:4 * ipt + b + 1, 0:D].broadcast_to([32, D]))
            dl = dma("sync", LQ[pp], wstat[:, base + 2048:base + 3072],
                     wdT_h[ipt * 128:(ipt + 1) * 128, 0:D])
            W("vector", LQ[pp], dl)
            if wd_stage_dma[pp]:
                W("vector", "wr", wd_stage_dma[pp])
            op("vector", lambda e, base=base: e.tensor_tensor(
                itmpd[:, 0:1024], wstat[:, base + 2048:base + 3072],
                wstat[:, base:base + 1024], ALU.mult))
            sexp_prev_use[pp] = op("vector", lambda e, base=base, pp=pp: e.tensor_tensor(
                spst[:, pp * 1024:(pp + 1) * 1024], itmpd[:, 0:1024],
                wstat[:, base + 1024:base + 2048], ALU.mult), "vv")
            W("gpsimd", "vv", sexp_prev_use[pp])
            wd_stage_dma[pp] = dma("gpsimd", "wr", sp_wd[:, ipt * 1024:(ipt + 1) * 1024],
                                   spst[:, pp * 1024:(pp + 1) * 1024])

        for ipt in range(NI):
            for cw in range(NCW):
                pq = c1_tile % 2
                if gi_dma[pq]:
                    W("sync", "aa")
                    W("sync", "vv", prev_g2_tt)
                gi_dma[pq] = dma("sync", LQ[pq],
                                 cst[:, 5120 + pq * 1024: 5120 + (pq + 1) * 1024],
                                 sp_g[ipt * 128:(ipt + 1) * 128, cw * CW:(cw + 1) * CW])
                W("scalar", LQ[pq], gi_dma[pq])
                if prev_g2_tt:
                    W("scalar", "vv", prev_g2_tt)
                op("scalar", lambda e, pq=pq: e.activation(
                    itmp[:, 0:1024], cst[:, 5120 + pq * 1024: 5120 + (pq + 1) * 1024],
                    ACTF.Copy, bias=0.0, scale=cols[:, 1:2]))
                op("scalar", lambda e: e.activation(
                    ftmp[:, 0:1024], itmp[:, 0:1024], ACTF.Sigmoid, bias=0.0, scale=cols[:, 0:1]))
                gi_consumed[pq] = op("scalar", lambda e: e.activation(
                    itmp[:, 1024:2048], ftmp[:, 0:1024], ACTF.Copy, bias=0.0,
                    scale=ccol[:, 0:1]), "aa")
                W("vector", "aa", gi_consumed[pq])
                op("vector", lambda e: e.tensor_scalar(
                    cst[:, 2048:3072], itmp[:, 1024:2048], 65535.0, 1.0 / 131072.0, ALU.min, ALU.mult))
                if g2_slot_dma[pq]:
                    W("vector", "wr", g2_slot_dma[pq])
                prev_g2_tt = op("vector", lambda e, pq=pq: e.tensor_tensor(
                    cst[:, 3072 + pq * 1024: 3072 + (pq + 1) * 1024],
                    itmp[:, 0:1024], cst[:, 2048:3072], ALU.mult), "vv")
                g2_slot_red[pq] = op("vector", lambda e, pq=pq, n=c1_tile: e.reduce_max(
                    colstrip[:, n:n + 1], cst[:, 3072 + pq * 1024: 3072 + (pq + 1) * 1024],
                    axis=AXX, apply_absolute_value=True), "vv")
                W("gpsimd", "vv", g2_slot_red[pq])
                g2_slot_dma[pq] = dma("gpsimd", "wr",
                                      sp_g2[ipt * 128:(ipt + 1) * 128, cw * CW:(cw + 1) * CW],
                                      cst[:, 3072 + pq * 1024: 3072 + (pq + 1) * 1024])
                c1_tile += 1

        ckpt('P6')
        def derive3():
            vch(lambda e: e.tensor_tensor(
                scal[0:1, 21:22], scal[0:1, 12:13], scal[0:1, 20:21], ALU.mult))
            vch(lambda e: e.tensor_scalar(
                scal[0:1, 27:28], scal[0:1, 21:22], 1.0 / QMAX, 1e-12, ALU.mult, ALU.max))
            vch(lambda e: e.reciprocal(scal2[0:1, 29:31], scal[0:1, 27:29]))
            vch(lambda e: e.tensor_copy(scal[0:1, 29:30], scal2[0:1, 29:30]))
            vch(lambda e: e.tensor_tensor(
                scal[0:1, 30:31], scal[0:1, 20:21], scal[0:1, 29:30], ALU.mult))
            return vch(lambda e: e.tensor_copy(scal[0:1, 31:32], scal[0:1, 22:23]))
        sync3_done = scale_sync([(0, 96)], ccs_i[2], ccs_o[2], derive3, [30, 31])

        ckpt('sync3')
        W("scalar", "vv", sync3_done)
        W("sync", "wr")
        oo_slot_dma = [C["wr"], C["wr"]]
        oo_slot_red = [0, 0]
        in_consumed = [0, 0]
        prev_oo_tt = 0
        c2_tile = 0
        for ipt in range(NI):
            W("sync", "wr")
            for cw in range(NCW):
                pq = c2_tile % 2
                if c2_tile >= 2:
                    W("sync", "aa")
                d1 = dma("sync", LQ[pq], cst[:, pq * 1024:(pq + 1) * 1024],
                         sp_g2[ipt * 128:(ipt + 1) * 128, cw * CW:(cw + 1) * CW])
                d2 = dma("sync", LQ[pq], cst[:, 2048 + pq * 1024: 2048 + (pq + 1) * 1024],
                         sp_up[ipt * 128:(ipt + 1) * 128, cw * CW:(cw + 1) * CW])
                W("scalar", LQ[pq], d2)
                if prev_oo_tt:
                    W("scalar", "vv", prev_oo_tt)
                op("scalar", lambda e, pq=pq: e.activation(
                    itmp[:, 0:1024], cst[:, pq * 1024:(pq + 1) * 1024],
                    ACTF.Copy, bias=0.0, scale=cols[:, 0:1]))
                in_consumed[pq] = op("scalar", lambda e, pq=pq: e.activation(
                    itmp[:, 1024:2048], cst[:, 2048 + pq * 1024: 2048 + (pq + 1) * 1024],
                    ACTF.Copy, bias=0.0, scale=cols[:, 1:2]), "aa")
                W("vector", "aa", in_consumed[pq])
                W("vector", "wr", oo_slot_dma[pq])
                prev_oo_tt = op("vector", lambda e, pq=pq: e.scalar_tensor_tensor(
                    cst[:, 4096 + pq * 1024: 4096 + (pq + 1) * 1024],
                    itmp[:, 0:1024], 1.0 / 65536.0, itmp[:, 1024:2048],
                    ALU.mult, ALU.mult), "vv")
                oo_slot_red[pq] = op("vector", lambda e, pq=pq, n=c2_tile: e.reduce_max(
                    colstrip[:, n:n + 1], cst[:, 4096 + pq * 1024: 4096 + (pq + 1) * 1024],
                    axis=AXX, apply_absolute_value=True), "vv")
                W("gpsimd", "vv", oo_slot_red[pq])
                oo_slot_dma[pq] = dma("gpsimd", "wr",
                                      sp_o[ipt * 128:(ipt + 1) * 128, cw * CW:(cw + 1) * CW],
                                      cst[:, 4096 + pq * 1024: 4096 + (pq + 1) * 1024])
                c2_tile += 1
                if c2_tile == 3:
                    ckpt("P7a")
                if c2_tile == 48:
                    ckpt("P7b")

        ckpt('P7')
        def derive4():
            vch(lambda e: e.scalar_tensor_tensor(
                scal[0:1, 21:22], scal[0:1, 27:28], 65536.0, scal[0:1, 18:19],
                ALU.mult, ALU.mult))
            vch(lambda e: e.tensor_tensor(
                scal[0:1, 13:14], scal[0:1, 12:13], scal[0:1, 21:22], ALU.mult))
            vch(lambda e: e.tensor_scalar(
                scal[0:1, 15:16], scal[0:1, 13:14], 1.0 / QMAX, 1e-12, ALU.mult, ALU.max))
            vch(lambda e: e.reciprocal(scal2[0:1, 24:26], scal[0:1, 15:17]))
            vch(lambda e: e.tensor_copy(scal[0:1, 24:25], scal2[0:1, 24:25]))
            return vch(lambda e: e.tensor_tensor(
                scal[0:1, 14:15], scal[0:1, 21:22], scal[0:1, 24:25], ALU.mult))
        sync4_done = scale_sync([(0, 96)], ccs_i[3], ccs_o[3], derive4, [14, 15])

        ckpt('sync4')
        W("sync", "wr")
        W("sync", "aa")
        o_in_dma = [0, 0]
        prev_ev = [0, 0]
        vro_prev = [0, 0]
        ro_consumed = [0, 0]
        wd_rd = [0, 0]
        out_dma = 0
        hbank = ps[:7] + [psT]
        stats2_read = sync4_done
        mst2_h = {}
        for sw in range(NSW):
            W("sync", "wr")
            for ipt in range(NI):
                pq = ipt % 2
                if wd_rd[pq]:
                    W("sync", "tv", ro_consumed[pq])
                wd_rd[pq] = dma("sync", LQ[pq], wdst[:, pq * 1024:(pq + 1) * 1024],
                                sp_wd[:, ipt * 1024:(ipt + 1) * 1024])
                if vro_prev[pq]:
                    W("sync", "vv")
                d2 = dma("sync", LQ[pq], cst[:, pq * 512:(pq + 1) * 512],
                         sp_o[ipt * 128:(ipt + 1) * 128, sw * SW:(sw + 1) * SW])
                W("vector", LQ[pq], max(wd_rd[pq], d2))
                if ro_consumed[pq]:
                    W("vector", "tv", ro_consumed[pq])
                op("vector", lambda e, pq=pq: e.tensor_scalar(
                    itmp[:, 0:512], cst[:, pq * 512:(pq + 1) * 512], cols[:, 0:1], None, ALU.mult))
                vro_prev[pq] = op("vector", lambda e, pq=pq: e.tensor_scalar(
                    rost[:, pq * 512:(pq + 1) * 512], itmp[:, 0:512], 1.0, None, ALU.mult), "vv")
                W("tensor", "vv", vro_prev[pq])
                if sw > 0 and ipt == 0:
                    W("tensor", "vv", stats2_read)
                    W("tensor", "tv", mst2_h[7])
                for h in range(8):
                    ro_consumed[pq] = op("tensor", lambda e, h=h, ipt=ipt, pq=pq: e.matmul(
                        hbank[h][:, :],
                        wdst[:, pq * 1024 + h * 128: pq * 1024 + h * 128 + 128],
                        rost[:, pq * 512:(pq + 1) * 512],
                        start=(ipt == 0), stop=(ipt == NI - 1), skip_group_check=True),
                        "tv" if h == 7 else None)
            dmm = ro_consumed[(NI - 1) % 2]
            W("vector", "tv", dmm)
            ev = {}
            for h in range(8):
                xs1 = 2048 + (h % 2) * 512
                if h >= 2:
                    W("sync", "vv", ev[h - 2])
                elif prev_ev[h]:
                    W("sync", "vv", prev_ev[h])
                x1ld = dma("sync", "lm", wstat[:, xs1:xs1 + 512],
                           sp_x1[h * 128:(h + 1) * 128, sw * SW:(sw + 1) * SW])
                if h >= 2:
                    W("vector", "tv", mst2_h[h - 2])
                W("vector", "lm", x1ld)
                op("vector", lambda e, h=h, xs1=xs1: e.scalar_tensor_tensor(
                    obuf[:, h * 512:(h + 1) * 512], hbank[h][:, :], cols[:, 1:2],
                    wstat[:, xs1:xs1 + 512], ALU.mult, ALU.add))
                if h == 0:
                    op("vector", lambda e, xs1=xs1: e.scalar_tensor_tensor(
                        x1st[:, 512:1024], hbank[0][:, :], cols[:, 1:2],
                        wstat[:, xs1:xs1 + 512], ALU.mult, ALU.add))
                if h == 1:
                    op("vector", lambda e, xs1=xs1: e.scalar_tensor_tensor(
                        ftmp[:, 512:1024], hbank[1][:, :], cols[:, 1:2],
                        wstat[:, xs1:xs1 + 512], ALU.mult, ALU.add))
                sqo = (h % 2) * 512
                ev[h] = op("vector", lambda e, h=h, sqo=sqo: e.tensor_tensor(
                    sqt[:, sqo:sqo + 512], obuf[:, h * 512:(h + 1) * 512],
                    obuf[:, h * 512:(h + 1) * 512], ALU.mult), "vv")
                if h >= 1:
                    hh = h - 1
                    W("tensor", "vv", ev[h])
                    op("tensor", lambda e, hh=hh: e.matmul(
                        ps[0][:, :], ones_b[:, :], obuf[:, hh * 512:(hh + 1) * 512],
                        start=(hh == 0), stop=(hh == 7), skip_group_check=True))
                    mst2_h[hh] = op("tensor", lambda e, hh=hh: e.matmul(
                        ps[1][:, :], ones_b[:, :], sqt[:, (hh % 2) * 512:(hh % 2) * 512 + 512],
                        start=(hh == 0), stop=(hh == 7), skip_group_check=True), "tv")
            W("tensor", "vv", ev[7])
            op("tensor", lambda e: e.matmul(
                ps[0][:, :], ones_b[:, :], obuf[:, 7 * 512:8 * 512],
                start=False, stop=True, skip_group_check=True))
            mst2_h[7] = op("tensor", lambda e: e.matmul(
                ps[1][:, :], ones_b[:, :], sqt[:, 512:1024],
                start=False, stop=True, skip_group_check=True), "tv")
            W("vector", "tv", mst2_h[7])
            op("vector", lambda e: e.tensor_scalar(mtile[:, :], ps[0][:, :], 1.0 / 1024.0, None, ALU.mult))
            op("vector", lambda e: e.tensor_tensor(ttile[:, :], mtile[:, :], mtile[:, :], ALU.mult))
            op("vector", lambda e: e.tensor_scalar(
                rtile[:, :], ps[1][:, :], 1.0 / 1024.0, EPS, ALU.mult, ALU.add))
            va = op("vector", lambda e: e.tensor_tensor(
                ttile[:, :], rtile[:, :], ttile[:, :], ALU.subtract), "vv")
            stats2_read = va
            W("scalar", "vv", va)
            sqv = op("scalar", lambda e: e.activation(rtile[:, :], ttile[:, :], ACTF.Sqrt,
                                                      bias=0.0, scale=1.0), "aa")
            W("vector", "aa", sqv)
            op("vector", lambda e: e.reciprocal(ttile[:, :], rtile[:, :]))
            op("vector", lambda e: e.tensor_tensor(mrtile[:, :], mtile[:, :], ttile[:, :], ALU.mult))
            ap_last = 0
            for h in range(8):
                if h == 0:
                    op("vector", lambda e: e.tensor_tensor(
                        ftmp[:, 0:512], x1st[:, 512:1024], ttile[:, :], ALU.mult))
                elif h == 1:
                    op("vector", lambda e: e.tensor_tensor(
                        ftmp[:, 0:512], ftmp[:, 512:1024], ttile[:, :], ALU.mult))
                else:
                    xs2 = 4096 + (h % 2) * 512
                    if ap_last:
                        W("sync", "vv", ap_last)
                    x2ld = dma("sync", "lm", wstat[:, xs2:xs2 + 512],
                               sp_x1[h * 128:(h + 1) * 128, sw * SW:(sw + 1) * SW])
                    W("vector", "lm", x2ld)
                    op("vector", lambda e, h=h, xs2=xs2: e.scalar_tensor_tensor(
                        ftmp[:, 512:1024], hbank[h][:, :], cols[:, 1:2],
                        wstat[:, xs2:xs2 + 512], ALU.mult, ALU.add))
                    op("vector", lambda e: e.tensor_tensor(
                        ftmp[:, 0:512], ftmp[:, 512:1024], ttile[:, :], ALU.mult))
                if out_dma:
                    W("vector", "wr", out_dma)
                op("vector", lambda e: e.tensor_tensor(
                    x1st[:, 0:512], ftmp[:, 0:512], mrtile[:, :], ALU.subtract))
                vap = op("vector", lambda e: e.reduce_max(
                    rmcol[:, 7:8], x1st[:, 0:512], axis=AXX), "vv")
                ap_last = vap
                W("gpsimd", "vv", vap)
                out_dma = dma("gpsimd", "wr",
                              y_h[h * 128:(h + 1) * 128, sw * SW:(sw + 1) * SW],
                              x1st[:, 0:512])
            stats2_read = ap_last
            prev_ev = [ev[6], ev[7]]
        W("gpsimd", "wr")
        if debug == "lite":
            dma("gpsimd", "wr", dbg_x1[:, :], sp_x1[:, :])
            dma("gpsimd", "wr", dbg_a[:, :], A[:, :])
            dma("gpsimd", "wr", dbg_up[:, :], sp_up[:, :])
            dma("gpsimd", "wr", dbg_g2[:, :], sp_g2[:, :])
            dma("gpsimd", "wr", dbg_o[:, :], sp_o[:, :])
            W("gpsimd", "wr")
        if debug is True:
            dma("gpsimd", "wr", dbg_x1[:, :], sp_x1[:, :])
            dma("gpsimd", "wr", dbg_up[:, :], sp_up[:, :])
            dma("gpsimd", "wr", dbg_g[:, :], sp_g[:, :])
            dma("gpsimd", "wr", dbg_g2[:, :], sp_g2[:, :])
            dma("gpsimd", "wr", dbg_o[:, :], sp_o[:, :])
            dma("gpsimd", "wr", dbg_wd[:, :], sp_wd[:, :])
            dma("gpsimd", "wr", dbg_scal[:, :], scal[:, :])
            dma("gpsimd", "wr", dbg_cols[:, :], cols[:, :])
            dma("gpsimd", "wr", dbg_m[:, :], wb2[:, :])
            dma("gpsimd", "wr", dbg_tu[:, :], dT_up[:, :])
            dma("gpsimd", "wr", dbg_iu[:, :], dI_up[:, :])
            dma("gpsimd", "wr", dbg_tg[:, :], dT_g[:, :])
            dma("gpsimd", "wr", dbg_a[:, :], A[:, :])
            dma("gpsimd", "wr", dbg_vk[:, :], vk_b[:, :])
            W("gpsimd", "wr")

    except _Stop:
        pass

    with nc.Block() as block:
        @block.sync
        def _(eng):
            for f in streams["sync"]:
                f(eng)

        @block.vector
        def _(eng):
            for f in streams["vector"]:
                f(eng)

        @block.scalar
        def _(eng):
            for f in streams["scalar"]:
                f(eng)

        @block.tensor
        def _(eng):
            for f in streams["tensor"]:
                f(eng)

        @block.gpsimd
        def _(eng):
            for f in streams["gpsimd"]:
                f(eng)

    es.close()
    return nc


_NC_CACHE = None


def kernel(L0_attn_x, L0_attn_W_q, L0_attn_W_k, L0_attn_W_v, L0_attn_W_o,
           L0_ffn_w_gate, L0_ffn_w_up, L0_ffn_w_down):
    global _NC_CACHE
    x = np.asarray(L0_attn_x, dtype=np.float32)
    W_q = np.asarray(L0_attn_W_q, dtype=np.float32)
    W_k = np.asarray(L0_attn_W_k, dtype=np.float32)
    W_v = np.asarray(L0_attn_W_v, dtype=np.float32)
    W_o = np.asarray(L0_attn_W_o, dtype=np.float32)
    w_g = np.asarray(L0_ffn_w_gate, dtype=np.float32)
    w_u = np.asarray(L0_ffn_w_up, dtype=np.float32)
    w_d = np.asarray(L0_ffn_w_down, dtype=np.float32)

    wqT = np.ascontiguousarray(W_q.T)
    wkT = np.ascontiguousarray(W_k.T)
    wvT = np.ascontiguousarray(W_v.T)
    wo_pk = np.ascontiguousarray(
        W_o.T.reshape(16, 64, 1024).transpose(1, 0, 2).reshape(64, 16 * 1024))
    wupT = np.ascontiguousarray(w_u.T)
    wgT = np.ascontiguousarray(w_g.T)
    wdT = np.ascontiguousarray(w_d.T)
    ones_b = np.ones((128, 128), dtype=BF)
    ident = np.eye(128, dtype=np.float32)
    ones_row = np.ones((1, 128), dtype=np.float32)

    if _NC_CACHE is None:
        _NC_CACHE = build_nc()
    nc = _NC_CACHE

    in_maps = []
    for c in range(8):
        b, half = c // 2, c % 2
        x0T = np.ascontiguousarray(x[b, half * T:(half + 1) * T, :].T)
        in_maps.append({
            "x0T": x0T, "wqT": wqT, "wkT": wkT, "wvT": wvT, "wo_pk": wo_pk,
            "wupT": wupT, "wgT": wgT, "wdT": wdT,
            "wup_n": np.ascontiguousarray(w_u), "wg_n": np.ascontiguousarray(w_g),
            "wd_n": np.ascontiguousarray(w_d),
            "ones_b": ones_b, "ident": ident, "ones_row": ones_row,
        })

    res = run_bass_kernel_spmd(nc, in_maps, core_ids=list(range(8)))
    out = np.empty((4, 8192, 1024), dtype=np.float32)
    for c in range(8):
        b, half = c // 2, c % 2
        out[b, half * T:(half + 1) * T, :] = res.results[c]["y"].T
    return out


# version 16 (final)
# speedup vs baseline: 1.0191x; 1.0191x over previous
import numpy as np
import ml_dtypes
from contextlib import ExitStack

import concourse.bass as bass
import concourse.mybir as mybir
from concourse.bass_utils import run_bass_kernel_spmd

F32 = mybir.dt.float32
I32 = mybir.dt.int32
BF16 = mybir.dt.bfloat16
F16 = mybir.dt.float16
ALU = mybir.AluOpType
ACTF = mybir.ActivationFunctionType
AXX = mybir.AxisListType.X
BF = ml_dtypes.bfloat16

D = 1024
NM = 8
T = 4096
SW = 512
NSW = 8
NT = 32
NH = 16
I3 = 3072
NI = 24
IC = 1024
NIC = 3
CW = 1024
NCW = 4
QMAX = 32767.0
EPS = 1e-5


def build_nc(stop=None, debug=False):
    nc = bass.Bass("TRN2", target_bir_lowering=False)

    x0T_h = nc.dram_tensor("x0T", [D, T], F32, kind="ExternalInput").ap()
    wqT_h = nc.dram_tensor("wqT", [D, D], F32, kind="ExternalInput").ap()
    wkT_h = nc.dram_tensor("wkT", [D, D], F32, kind="ExternalInput").ap()
    wvT_h = nc.dram_tensor("wvT", [D, D], F32, kind="ExternalInput").ap()
    wo_pk_h = nc.dram_tensor("wo_pk", [64, NH * D], F32, kind="ExternalInput").ap()
    wupT_h = nc.dram_tensor("wupT", [D, I3], F32, kind="ExternalInput").ap()
    wgT_h = nc.dram_tensor("wgT", [D, I3], F32, kind="ExternalInput").ap()
    wdT_h = nc.dram_tensor("wdT", [I3, D], F32, kind="ExternalInput").ap()
    wup_n_h = nc.dram_tensor("wup_n", [I3, D], F32, kind="ExternalInput").ap()
    wg_n_h = nc.dram_tensor("wg_n", [I3, D], F32, kind="ExternalInput").ap()
    wd_n_h = nc.dram_tensor("wd_n", [D, I3], F32, kind="ExternalInput").ap()
    ones_b_h = nc.dram_tensor("ones_b", [128, 128], BF16, kind="ExternalInput").ap()
    ident_h = nc.dram_tensor("ident", [128, 128], F32, kind="ExternalInput").ap()
    ones_row_h = nc.dram_tensor("ones_row", [1, 128], F32, kind="ExternalInput").ap()
    y_h = nc.dram_tensor("y", [D, T], F32, kind="ExternalOutput").ap()
    if debug == "lite":
        dbg_x1 = nc.dram_tensor("dbg_x1", [D, T], F32, kind="ExternalOutput").ap()
        dbg_a = nc.dram_tensor("dbg_a", [128, NM * T], BF16, kind="ExternalOutput").ap()
        dbg_up = nc.dram_tensor("dbg_up", [I3, T], F16, kind="ExternalOutput").ap()
        dbg_g2 = nc.dram_tensor("dbg_g2", [I3, T], F16, kind="ExternalOutput").ap()
        dbg_o = nc.dram_tensor("dbg_o", [I3, T], F16, kind="ExternalOutput").ap()
    if debug is True:
        dbg_x1 = nc.dram_tensor("dbg_x1", [D, T], F32, kind="ExternalOutput").ap()
        dbg_up = nc.dram_tensor("dbg_up", [I3, T], F16, kind="ExternalOutput").ap()
        dbg_g = nc.dram_tensor("dbg_g", [I3, T], F16, kind="ExternalOutput").ap()
        dbg_g2 = nc.dram_tensor("dbg_g2", [I3, T], F16, kind="ExternalOutput").ap()
        dbg_o = nc.dram_tensor("dbg_o", [I3, T], F16, kind="ExternalOutput").ap()
        dbg_wd = nc.dram_tensor("dbg_wd", [128, NI * D], BF16, kind="ExternalOutput").ap()
        dbg_scal = nc.dram_tensor("dbg_scal", [1, 32], F32, kind="ExternalOutput").ap()
        dbg_cols = nc.dram_tensor("dbg_cols", [128, 8], F32, kind="ExternalOutput").ap()
        dbg_m = nc.dram_tensor("dbg_m", [128, 8192], BF16, kind="ExternalOutput").ap()
        dbg_tu = nc.dram_tensor("dbg_tu", [32, I3], F32, kind="ExternalOutput").ap()
        dbg_iu = nc.dram_tensor("dbg_iu", [32, I3], F32, kind="ExternalOutput").ap()
        dbg_tg = nc.dram_tensor("dbg_tg", [32, I3], F32, kind="ExternalOutput").ap()
        dbg_a = nc.dram_tensor("dbg_a", [128, NM * T], BF16, kind="ExternalOutput").ap()
        dbg_vk = nc.dram_tensor("dbg_vk", [64, 1024], BF16, kind="ExternalOutput").ap()
        dbg_m2 = nc.dram_tensor("dbg_m2", [128, 8192], BF16, kind="ExternalOutput").ap()

    sp_x1 = nc.dram_tensor("sp_x1", [D, T], F32, kind="Internal").ap()
    sp_up = nc.dram_tensor("sp_up", [I3, T], F16, kind="Internal").ap()
    sp_g = nc.dram_tensor("sp_g", [I3, T], F16, kind="Internal").ap()
    sp_g2 = nc.dram_tensor("sp_g2", [I3, T], F16, kind="Internal").ap()
    sp_o = nc.dram_tensor("sp_o", [I3, T], F16, kind="Internal").ap()
    sp_wd = nc.dram_tensor("sp_wd", [128, NI * D], BF16, kind="Internal").ap()
    dT_up = nc.dram_tensor("dT_up", [32, I3], F32, kind="Internal").ap()
    dI_up = nc.dram_tensor("dI_up", [32, I3], F32, kind="Internal").ap()
    dT_g = nc.dram_tensor("dT_g", [32, I3], F32, kind="Internal").ap()
    dI_g = nc.dram_tensor("dI_g", [32, I3], F32, kind="Internal").ap()
    dT_d = nc.dram_tensor("dT_d", [96, D], F32, kind="Internal").ap()
    dI_d = nc.dram_tensor("dI_d", [96, D], F32, kind="Internal").ap()
    tmpcol_t = nc.dram_tensor("tmpcol", [128, 1], F32, kind="Internal")
    tmpcol = tmpcol_t.ap()
    tmpcol_row = tmpcol_t.ap().flatten().unsqueeze(0)
    cckv_i = nc.dram_tensor("cckv_i", [64, 1024], F32, kind="Internal").ap()
    cckv_o = nc.dram_tensor("cckv_o", [64, 1024], F32, kind="Internal").ap()
    ccs_i = [nc.dram_tensor(f"ccs_i{k}", [1, 2], F32, kind="Internal").ap() for k in range(4)]
    ccs_o = [nc.dram_tensor(f"ccs_o{k}", [1, 2], F32, kind="Internal").ap() for k in range(4)]

    es = ExitStack()
    ent = es.enter_context

    A = ent(nc.sbuf_tensor("A", [128, NM * T], BF16))
    wb1 = ent(nc.sbuf_tensor("wb1", [128, 8192], BF16))
    wb2 = ent(nc.sbuf_tensor("wb2", [128, 8192], BF16))
    obuf = ent(nc.sbuf_tensor("obuf", [128, 8192], BF16))
    wstat = ent(nc.sbuf_tensor("wstat", [128, 6144], F32))
    cst = ent(nc.sbuf_tensor("cst", [128, 8192], F16))
    k_sb = ent(nc.sbuf_tensor("k_sb", [128, 1024], BF16))
    v_sb = ent(nc.sbuf_tensor("v_sb", [128, 1024], BF16))
    vk_b = ent(nc.sbuf_tensor("vk_b", [64, 1024], BF16))
    sqt = ent(nc.sbuf_tensor("sqt", [128, 1024], BF16))
    mtile = ent(nc.sbuf_tensor("mtile", [128, 512], F32))
    ttile = ent(nc.sbuf_tensor("ttile", [128, 512], F32))
    rtile = ent(nc.sbuf_tensor("rtile", [128, 512], F32))
    mrtile = ent(nc.sbuf_tensor("mrtile", [128, 512], F32))
    itmp = ent(nc.sbuf_tensor("itmp", [128, 2048], I32))
    itmpd = ent(nc.sbuf_tensor("itmpd", [128, 1024], I32))
    ftmp = ent(nc.sbuf_tensor("ftmp", [128, 1024], F32))
    spst = ent(nc.sbuf_tensor("spst", [128, 2048], BF16))
    rost = ent(nc.sbuf_tensor("rost", [128, 1024], BF16))
    wdst = ent(nc.sbuf_tensor("wdst", [128, 2048], BF16))
    strip = ent(nc.sbuf_tensor("strip", [96, 256], F32))
    stt_s = ent(nc.sbuf_tensor("stt_s", [128, 128], F32))
    colstrip = ent(nc.sbuf_tensor("colstrip", [128, 512], F32))
    rmcol = ent(nc.sbuf_tensor("rmcol", [128, 8], F32))
    cols = ent(nc.sbuf_tensor("cols", [128, 8], F32))
    scal = ent(nc.sbuf_tensor("scal", [1, 32], F32))
    rowb = ent(nc.sbuf_tensor("rowb", [1, 128], F32))
    ones_b = ent(nc.sbuf_tensor("onesb", [128, 128], BF16))
    ident = ent(nc.sbuf_tensor("identt", [128, 128], F32))
    ones_row = ent(nc.sbuf_tensor("onesr", [1, 128], F32))
    ccol = ent(nc.sbuf_tensor("ccol", [128, 2], F32))
    scal2 = ent(nc.sbuf_tensor("scal2", [1, 32], F32))
    stt_i = ent(nc.sbuf_tensor("stt_i", [128, 128], F32))
    x1st = ent(nc.sbuf_tensor("x1st", [128, 1024], F32))
    wstg = x1st

    ps = [ent(nc.psum_tensor(f"ps{i}", [128, 512], F32)) for i in range(7)]
    psT = ent(nc.psum_tensor("psT", [128, 512], F32))

    semnames = ["ldx", "lst", "lq0", "lq1", "lm", "wr", "tv", "vv", "aa", "gg"]
    sems = {n: ent(nc.semaphore(n)) for n in semnames}
    C = {n: 0 for n in semnames}
    streams = {k: [] for k in ("sync", "vector", "scalar", "tensor", "gpsimd")}

    def em(eng, f):
        streams[eng].append(f)

    def W(eng, sem, val=None):
        v = C[sem] if val is None else val
        assert v <= C[sem], (eng, sem, v, C[sem])
        if v > 0:
            em(eng, lambda e, s=sems[sem], v=v: e.wait_ge(s, v))

    def dma(eng, sem, out, in_):
        C[sem] += 16
        em(eng, lambda e, o=out, i=in_, s=sems[sem]: e.dma_start(out=o, in_=i).then_inc(s, 16))
        return C[sem]

    def op(eng, f, sem=None):
        if sem is None:
            em(eng, lambda e, f=f: f(e))
            return None
        C[sem] += 1
        em(eng, lambda e, f=f, s=sems[sem]: f(e).then_inc(s, 1))
        return C[sem]

    LQ = ["lq0", "lq1"]

    def vch(f):
        v = op("vector", f, "vv")
        W("vector", "vv", v)
        return v

    class _Stop(Exception):
        pass

    def ckpt(name):
        if stop == name:
            raise _Stop()

    try:
        op("vector", lambda e: e.memset(ccol[:, 0:1], 65536.0))
        dma("sync", "ldx", scal[0:1, 0:32], ones_row_h[0:1, 0:32])
        dma("sync", "ldx", scal2[0:1, 0:32], ones_row_h[0:1, 0:32])
        dma("sync", "ldx", ones_b[:], ones_b_h[:, :])
        dma("sync", "ldx", ident[:], ident_h[:, :])
        dma("sync", "ldx", ones_row[:], ones_row_h[:, :])

        cast_v = 0
        for mc in range(NM):
            for hh in range(2):
                slot = ((mc * 2 + hh) % 2) * 2048
                if mc * 2 + hh >= 2:
                    W("sync", "vv", cast_v - 1)
                dl = dma("sync", "ldx", wstat[:, slot:slot + 2048],
                         x0T_h[mc * 128:(mc + 1) * 128, hh * 2048:(hh + 1) * 2048])
                W("vector", "ldx", dl)
                cast_v = op("vector", lambda e, mc=mc, hh=hh, slot=slot: e.tensor_copy(
                    A[:, mc * T + hh * 2048: mc * T + (hh + 1) * 2048],
                    wstat[:, slot:slot + 2048]), "vv")
        for (w_h, dst) in ((wkT_h, wb1), (wvT_h, wb2)):
            for mc in range(NM):
                slot = (mc % 2) * 2048
                W("sync", "vv", cast_v - 1)
                dl = dma("sync", "ldx", wstat[:, slot:slot + 1024],
                         w_h[mc * 128:(mc + 1) * 128, :])
                W("vector", "ldx", dl)
                cast_v = op("vector", lambda e, mc=mc, dst=dst, slot=slot: e.tensor_copy(
                    dst[:, mc * 1024:(mc + 1) * 1024], wstat[:, slot:slot + 1024]), "vv")
        p0_cast_done = cast_v

        ckpt('P0')
        W("tensor", "vv", p0_cast_done)
        aev = 0
        kv_last = 0
        tvp = 0
        for t in range(NT):
            if t > 0:
                W("tensor", "aa", aev)
            for mc in range(NM):
                lhs = A[:, mc * T + t * 128: mc * T + t * 128 + 128]
                for s2 in range(2):
                    op("tensor", lambda e, lhs=lhs, mc=mc, s2=s2: e.matmul(
                        ps[s2][:, :], lhs,
                        wb1[:, mc * 1024 + s2 * 512: mc * 1024 + (s2 + 1) * 512],
                        start=(mc == 0), stop=(mc == NM - 1)))
                    f2 = (lambda e, lhs=lhs, mc=mc, s2=s2: e.matmul(
                        ps[2 + s2][:, :], lhs,
                        wb2[:, mc * 1024 + s2 * 512: mc * 1024 + (s2 + 1) * 512],
                        start=(mc == 0), stop=(mc == NM - 1)))
                    if mc == NM - 1 and s2 == 1:
                        tvp = op("tensor", f2, "tv")
                    else:
                        op("tensor", f2)
            W("scalar", "tv", tvp)
            if t > 0:
                W("scalar", "tv", kv_last)
            op("scalar", lambda e: e.activation(k_sb[:, 0:512], ps[0][:, :], ACTF.Copy, bias=0.0, scale=1.0))
            op("scalar", lambda e: e.activation(k_sb[:, 512:1024], ps[1][:, :], ACTF.Copy, bias=0.0, scale=1.0))
            op("scalar", lambda e: e.activation(v_sb[:, 0:512], ps[2][:, :], ACTF.Copy, bias=0.0, scale=1.0))
            aev = op("scalar", lambda e: e.activation(v_sb[:, 512:1024], ps[3][:, :], ACTF.Copy, bias=0.0, scale=1.0), "aa")
            W("tensor", "aa", aev)
            for h in range(NH):
                bank = 4 + h // 8
                c0 = (h % 8) * 64
                f = (lambda e, h=h, bank=bank, c0=c0, t=t: e.matmul(
                    ps[bank][0:64, c0:c0 + 64], v_sb[:, h * 64:(h + 1) * 64],
                    k_sb[:, h * 64:(h + 1) * 64],
                    start=(t == 0 and (h % 8) == 0), stop=(t == NT - 1), skip_group_check=True))
                if h == NH - 1:
                    kv_last = op("tensor", f, "tv")
                else:
                    op("tensor", f)

        ckpt('P1')
        prev_stat_v = p0_cast_done
        prev_tr_t = 0

        def emit_stat_tile(dl, slot, tw, nb, s_off, i_off):
            W("vector", "lst", dl)
            if prev_tr_t:
                W("vector", "tv", prev_tr_t)
            vch(lambda e, slot=slot, tw=tw, nb=nb: e.reduce_max(
                stt_s[:, 0:nb], wstat[:, slot:slot + tw].rearrange("p (b u) -> p b u", u=32),
                axis=AXX, apply_absolute_value=True))
            vch(lambda e, nb=nb, s_off=s_off: e.tensor_scalar(
                stt_s[:, s_off:s_off + nb], stt_s[:, 0:nb], 1.0 / 7.0, 1e-12, ALU.mult, ALU.max))
            return vch(lambda e, nb=nb, s_off=s_off, i_off=i_off: e.reciprocal(
                stt_i[:, i_off:i_off + nb], stt_s[:, s_off:s_off + nb]))

        prev_strip_dma = [0]

        def emit_stat_store(vguard, nb, s_off, i_off, dTd, dId, r0, c0):
            W("tensor", "vv", vguard)
            op("tensor", lambda e, nb=nb, s_off=s_off: e.transpose(
                psT[0:nb, 0:128], stt_s[:, s_off:s_off + nb], ident[:]))
            t1 = op("tensor", lambda e, nb=nb, i_off=i_off: e.transpose(
                psT[0:nb, 128:256], stt_i[:, i_off:i_off + nb], ident[:]), "tv")
            W("vector", "tv", t1)
            if prev_strip_dma[0]:
                W("vector", "wr", prev_strip_dma[0])
            op("vector", lambda e, nb=nb: e.tensor_copy(strip[0:nb, 0:128], psT[0:nb, 0:128]))
            vg = op("vector", lambda e, nb=nb: e.tensor_copy(strip[0:nb, 128:256], psT[0:nb, 128:256]), "vv")
            W("gpsimd", "vv", vg)
            dma("gpsimd", "wr", dTd[r0:r0 + nb, c0:c0 + 128], strip[0:nb, 0:128])
            prev_strip_dma[0] = dma("gpsimd", "wr", dId[r0:r0 + nb, c0:c0 + 128], strip[0:nb, 128:256])
            return t1

        for (wh, npt, tw, nb, dTd, dId) in ((wup_n_h, NI, 1024, 32, dT_up, dI_up),
                                            (wg_n_h, NI, 1024, 32, dT_g, dI_g)):
            for pt in range(npt):
                slot = (pt % 2) * 1536
                W("sync", "vv", prev_stat_v)
                dl = dma("sync", "lst", wstat[:, slot:slot + tw], wh[pt * 128:(pt + 1) * 128, 0:tw])
                prev_stat_v = emit_stat_tile(dl, slot, tw, nb, 32, 64)
                prev_tr_t = emit_stat_store(prev_stat_v, nb, 32, 64, dTd, dId, 0, pt * 128)
        for pt in range(8):
            for hh in range(2):
                slot = (hh % 2) * 1536
                W("sync", "vv", prev_stat_v)
                dl = dma("sync", "lst", wstat[:, slot:slot + 1536],
                         wd_n_h[pt * 128:(pt + 1) * 128, hh * 1536:(hh + 1) * 1536])
                prev_stat_v = emit_stat_tile(dl, slot, 1536, 48, 48, 0)
                prev_tr_t = emit_stat_store(prev_stat_v, 48, 48, 0, dT_d, dI_d, hh * 48, pt * 128)
        stats_tables_done = C["wr"]

        ckpt('stats')
        W("vector", "tv", kv_last)
        op("vector", lambda e: e.tensor_copy(ftmp[0:64, 0:512], ps[4][0:64, :]))
        v1 = op("vector", lambda e: e.tensor_copy(ftmp[0:64, 512:1024], ps[5][0:64, :]), "vv")
        ckpt("P2kv1")
        W("gpsimd", "vv", v1)
        dkv = dma("gpsimd", "wr", cckv_i[:, :], ftmp[0:64, :])
        ckpt("P2kv2")
        W("gpsimd", "wr", dkv)
        cc_kv = op("gpsimd", lambda e: e.collective_compute(
            "AllReduce", ALU.add, replica_groups=[[0, 1], [2, 3], [4, 5], [6, 7]],
            ins=[cckv_i[:, :]], outs=[cckv_o[:, :]]), "gg")
        ckpt("P2kv3")
        W("sync", "gg", cc_kv)
        dvk = dma("sync", "lm", ftmp[0:64, :], cckv_o[:, :])
        W("vector", "lm", dvk)
        vkc = op("vector", lambda e: e.tensor_copy(vk_b[:, :], ftmp[0:64, :]), "vv")
        ckpt("P2a")

        wq_done = 0
        for mc in range(NM):
            if wq_done:
                W("sync", "vv", wq_done)
            dl = dma("sync", "ldx", wstg[:, 0:1024], wqT_h[mc * 128:(mc + 1) * 128, :])
            W("vector", "ldx", dl)
            if mc == 0:
                W("vector", "tv", kv_last)
            wq_done = op("vector", lambda e, mc=mc: e.tensor_copy(
                wb1[:, mc * 1024:(mc + 1) * 1024], wstg[:, 0:1024]), "vv")

        ckpt("P2b")
        m_done = 0
        woc = wq_done
        modd_dma = [0, 0]
        for half in range(2):
            for hh in range(8):
                W("sync", "vv", woc)
                dl = dma("sync", "ldx", wstg[0:64, 0:1024],
                         wo_pk_h[0:64, (half * 8 + hh) * 1024:(half * 8 + hh + 1) * 1024])
                W("vector", "ldx", dl)
                woc = op("vector", lambda e, hh=hh: e.tensor_copy(
                    obuf[0:64, hh * 1024:(hh + 1) * 1024], wstg[0:64, 0:1024]), "vv")
            W("tensor", "vv", max(woc, vkc))
            for hh in range(8):
                h = half * 8 + hh
                j, beta = h // 2, h % 2
                b0 = (hh % 2) * 2
                if m_done:
                    W("tensor", "vv", m_done)
                mm = 0
                for s2 in range(2):
                    mm = op("tensor", lambda e, h=h, hh=hh, s2=s2, b0=b0: e.matmul(
                        ps[b0 + s2][0:64, :],
                        vk_b[0:64, h * 64:(h + 1) * 64],
                        obuf[0:64, hh * 1024 + s2 * 512: hh * 1024 + (s2 + 1) * 512],
                        start=True, stop=True), "tv" if s2 == 1 else None)
                W("vector", "tv", mm)
                if beta == 0:
                    for s2 in range(2):
                        m_done = op("vector", lambda e, j=j, s2=s2, b0=b0: e.tensor_scalar(
                            wb2[0:64, j * 1024 + s2 * 512: j * 1024 + (s2 + 1) * 512],
                            ps[b0 + s2][0:64, :], 0.125, None, ALU.mult), "vv")
                else:
                    for s2 in range(2):
                        if modd_dma[s2]:
                            W("vector", "wr", modd_dma[s2])
                        m_done = op("vector", lambda e, s2=s2, b0=b0: e.tensor_scalar(
                            spst[0:64, s2 * 512:(s2 + 1) * 512],
                            ps[b0 + s2][0:64, :], 0.125, None, ALU.mult), "vv")
                        W("gpsimd", "vv", m_done)
                        modd_dma[s2] = dma("gpsimd", "wr",
                                           wb2[64:128, j * 1024 + s2 * 512: j * 1024 + (s2 + 1) * 512],
                                           spst[0:64, s2 * 512:(s2 + 1) * 512])
        m_wr_done = C["wr"]

        W("tensor", "vv", m_done)
        W("tensor", "wr", m_wr_done)
        aq = {}
        vxr = {}
        mst_oc = {}
        x1_tile = 0
        x1_slot_dma = [0, 0]
        stats_read_done = 0
        for sw in range(NSW):
            for oc in range(8):
                bank = oc % 2
                if oc >= 2:
                    W("tensor", "aa", aq[oc - 2])
                elif sw > 0:
                    W("tensor", "aa", aq[6 + oc])
                for mc in range(NM):
                    op("tensor", lambda e, oc=oc, mc=mc, sw=sw, bank=bank: e.matmul(
                        ps[bank][:, :],
                        wb1[:, mc * 1024 + oc * 128: mc * 1024 + oc * 128 + 128],
                        A[:, mc * T + sw * SW: mc * T + (sw + 1) * SW],
                        start=(mc == 0), stop=(mc == NM - 1)),
                        "tv" if mc == NM - 1 else None)
                mmq = C["tv"]
                W("scalar", "tv", mmq)
                aq[oc] = op("scalar", lambda e, oc=oc, bank=bank: e.activation(
                    obuf[:, oc * 512:(oc + 1) * 512], ps[bank][:, :],
                    ACTF.Copy, bias=0.0, scale=1.0), "aa")
            W("tensor", "aa", aq[7])
            for oc in range(8):
                bank = 2 + oc % 2
                if oc >= 2:
                    W("tensor", "vv", vxr[oc - 2])
                elif sw > 0:
                    W("tensor", "vv", max(vxr[6 + oc], stats_read_done))
                for tch in range(8):
                    op("tensor", lambda e, oc=oc, tch=tch, bank=bank: e.matmul(
                        ps[bank][:, :],
                        wb2[:, tch * 1024 + oc * 128: tch * 1024 + oc * 128 + 128],
                        obuf[:, tch * 512:(tch + 1) * 512],
                        start=(tch == 0), stop=(tch == 7)),
                        "tv" if tch == 7 else None)
                mmo = C["tv"]
                W("vector", "tv", mmo)
                if oc >= 2:
                    W("vector", "tv", mst_oc[oc - 2])
                op("vector", lambda e, oc=oc, sw=sw, bank=bank: e.tensor_tensor(
                    wstat[:, oc * 512:(oc + 1) * 512],
                    ps[bank][:, :],
                    A[:, oc * T + sw * SW: oc * T + (sw + 1) * SW], ALU.add))
                op("vector", lambda e, oc=oc: e.tensor_copy(
                    obuf[:, 4096 + oc * 512: 4096 + (oc + 1) * 512],
                    wstat[:, oc * 512:(oc + 1) * 512]))
                sqo = (oc % 2) * 512
                vxr[oc] = op("vector", lambda e, oc=oc, sqo=sqo: e.tensor_tensor(
                    sqt[:, sqo:sqo + 512],
                    obuf[:, 4096 + oc * 512: 4096 + (oc + 1) * 512],
                    obuf[:, 4096 + oc * 512: 4096 + (oc + 1) * 512], ALU.mult), "vv")
                W("tensor", "vv", vxr[oc])
                op("tensor", lambda e, oc=oc, sw=sw: e.matmul(
                    ps[4][:, :], ones_b[:, :],
                    obuf[:, 4096 + oc * 512: 4096 + (oc + 1) * 512],
                    start=(oc == 0), stop=(oc == 7), skip_group_check=True))
                mst_oc[oc] = op("tensor", lambda e, oc=oc, sqo=sqo: e.matmul(
                    ps[5][:, :], ones_b[:, :], sqt[:, sqo:sqo + 512],
                    start=(oc == 0), stop=(oc == 7), skip_group_check=True), "tv")
            W("vector", "tv", mst_oc[7])
            op("vector", lambda e: e.tensor_scalar(mtile[:, :], ps[4][:, :], 1.0 / 1024.0, None, ALU.mult))
            op("vector", lambda e: e.tensor_tensor(ttile[:, :], mtile[:, :], mtile[:, :], ALU.mult))
            op("vector", lambda e: e.tensor_scalar(
                rtile[:, :], ps[5][:, :], 1.0 / 1024.0, EPS, ALU.mult, ALU.add))
            va = op("vector", lambda e: e.tensor_tensor(
                ttile[:, :], rtile[:, :], ttile[:, :], ALU.subtract), "vv")
            stats_read_done = va
            W("scalar", "vv", va)
            sqv = op("scalar", lambda e: e.activation(rtile[:, :], ttile[:, :], ACTF.Sqrt,
                                                      bias=0.0, scale=1.0), "aa")
            W("vector", "aa", sqv)
            op("vector", lambda e: e.reciprocal(ttile[:, :], rtile[:, :]))
            op("vector", lambda e: e.tensor_tensor(mrtile[:, :], mtile[:, :], ttile[:, :], ALU.mult))
            for oc in range(8):
                slot = x1_tile % 2
                op("vector", lambda e, oc=oc: e.tensor_tensor(
                    ftmp[:, 0:512],
                    wstat[:, oc * 512:(oc + 1) * 512], ttile[:, :], ALU.mult))
                if x1_slot_dma[slot]:
                    W("vector", "wr", x1_slot_dma[slot])
                op("vector", lambda e, slot=slot: e.tensor_tensor(
                    x1st[:, slot * 512:(slot + 1) * 512], ftmp[:, 0:512], mrtile[:, :], ALU.subtract))
                vap = op("vector", lambda e, slot=slot, n=x1_tile: e.reduce_max(
                    colstrip[:, n:n + 1], x1st[:, slot * 512:(slot + 1) * 512],
                    axis=AXX, apply_absolute_value=True), "vv")
                W("gpsimd", "vv", vap)
                x1_slot_dma[slot] = dma("gpsimd", "wr",
                                        sp_x1[oc * 128:(oc + 1) * 128, sw * SW:(sw + 1) * SW],
                                        x1st[:, slot * 512:(slot + 1) * 512])
                x1_tile += 1

        ckpt('P3')
        def scale_sync(colranges, cc_i, cc_o, derive, bcols):
            vred = 0
            for i, (s0, cnt) in enumerate(colranges):
                vred = op("vector", lambda e, s0=s0, cnt=cnt, i=i: e.reduce_max(
                    rmcol[:, i:i + 1], colstrip[:, s0:s0 + cnt], axis=AXX), "vv")
            prev_row_rd = 0
            for i in range(len(colranges)):
                W("gpsimd", "vv", vred)
                if prev_row_rd:
                    W("gpsimd", "lm", prev_row_rd)
                dcw = dma("gpsimd", "wr", tmpcol[:, :], rmcol[:, i:i + 1])
                W("sync", "wr", dcw)
                if i > 0:
                    W("sync", "vv", vred)
                prev_row_rd = dma("sync", "lm", rowb[:, :], tmpcol_row)
                W("vector", "lm", prev_row_rd)
                vred = op("vector", lambda e, i=i: e.reduce_max(
                    scal[0:1, 8 + i:9 + i], rowb[0:1, :], axis=AXX), "vv")
            k = len(colranges)
            W("gpsimd", "vv", vred)
            dci = dma("gpsimd", "wr", cc_i[0:1, 0:k], scal[0:1, 8:8 + k])
            W("gpsimd", "wr", dci)
            ccv = op("gpsimd", lambda e, cc_i=cc_i, cc_o=cc_o, k=k: e.collective_compute(
                "AllReduce", ALU.max, replica_groups=[[0, 1, 2, 3, 4, 5, 6, 7]],
                ins=[cc_i[0:1, 0:k]], outs=[cc_o[0:1, 0:k]]), "gg")
            W("sync", "gg", ccv)
            dro = dma("sync", "lm", scal[0:1, 12:12 + k], cc_o[0:1, 0:k])
            W("vector", "lm", dro)
            vd = derive()
            nb = len(bcols)
            b0 = bcols[0]
            assert bcols == list(range(b0, b0 + nb))
            W("tensor", "vv", vd)
            tb = op("tensor", lambda e, b0=b0, nb=nb: e.matmul(
                psT[:, 256:256 + nb], ones_row[0:1, :], scal[0:1, b0:b0 + nb],
                start=True, stop=True, skip_group_check=True), "tv")
            W("vector", "tv", tb)
            return op("vector", lambda e, nb=nb: e.tensor_copy(cols[:, 0:nb], psT[:, 256:256 + nb]), "vv")

        def derive1():
            vch(lambda e: e.tensor_scalar(
                scal[0:1, 16:17], scal[0:1, 12:13], 1.0 / QMAX, 1e-12, ALU.mult, ALU.max))
            vch(lambda e: e.reciprocal(scal2[0:1, 20:22], scal[0:1, 16:18]))
            return vch(lambda e: e.tensor_copy(scal[0:1, 17:18], scal2[0:1, 20:21]))
        sync1_done = scale_sync([(0, 64)], ccs_i[0], ccs_o[0], derive1, [16, 17])

        ckpt('sync1')
        W("sync", "wr")
        W("sync", "vv")
        if debug is True:
            m2dump = dma("gpsimd", "wr", dbg_m2[:, :], wb2[:, :])
            W("vector", "wr", m2dump)
        x1q_done = sync1_done
        rd_dma = [0, 0]
        for mc in range(NM):
            for cw in range(NCW):
                pslot = (mc * NCW + cw) % 2
                if rd_dma[pslot]:
                    W("sync", "vv", x1q_done)
                rd_dma[pslot] = dma("sync", LQ[pslot], wstat[:, pslot * 1024:(pslot + 1) * 1024],
                                    sp_x1[mc * 128:(mc + 1) * 128, cw * 1024:(cw + 1) * 1024])
                W("vector", LQ[pslot], rd_dma[pslot])
                op("vector", lambda e, pslot=pslot: e.tensor_scalar(
                    itmp[:, 0:1024], wstat[:, pslot * 1024:(pslot + 1) * 1024],
                    cols[:, 1:2], None, ALU.mult))
                x1q_done = op("vector", lambda e, mc=mc, cw=cw: e.tensor_scalar(
                    A[:, mc * T + cw * 1024: mc * T + (cw + 1) * 1024],
                    itmp[:, 0:1024], cols[:, 0:1], None, ALU.mult), "vv")

        spill_slot_dma = [0, 0, 0, 0]
        spill_slot_red = [0, 0, 0, 0]
        W("sync", "wr", stats_tables_done)
        W("sync", "vv")
        sexp_prev_use = [0, 0]
        mm_last_read_wb = 0
        utile = 0
        gtile = 0
        evac_mi = {0: [], 1: []}
        for ic in range(NIC):
            for mi, (wT_h, dT, dI, wbuf, spd) in enumerate(
                    ((wupT_h, dT_up, dI_up, wb1, sp_up), (wgT_h, dT_g, dI_g, wb2, sp_g))):
                for pc in range(NM):
                    pp = pc % 2
                    base = pp * 3072
                    if sexp_prev_use[pp]:
                        W("sync", "vv", sexp_prev_use[pp])
                    for b in range(4):
                        dma("sync", LQ[pp], wstat[32 * b:32 * (b + 1), base:base + 1024],
                            dI[4 * pc + b:4 * pc + b + 1, ic * IC:(ic + 1) * IC].broadcast_to([32, IC]))
                    for b in range(4):
                        dma("sync", LQ[pp], wstat[32 * b:32 * (b + 1), base + 1024:base + 2048],
                            dT[4 * pc + b:4 * pc + b + 1, ic * IC:(ic + 1) * IC].broadcast_to([32, IC]))
                    dl = dma("sync", LQ[pp], wstat[:, base + 2048:base + 3072],
                             wT_h[pc * 128:(pc + 1) * 128, ic * IC:(ic + 1) * IC])
                    W("vector", LQ[pp], dl)
                    if mm_last_read_wb:
                        W("vector", "tv", mm_last_read_wb)
                    op("vector", lambda e, base=base: e.tensor_tensor(
                        itmpd[:, 0:1024], wstat[:, base + 2048:base + 3072],
                        wstat[:, base:base + 1024], ALU.mult))
                    sexp_prev_use[pp] = op("vector", lambda e, base=base, pc=pc, wbuf=wbuf: e.tensor_tensor(
                        wbuf[:, pc * 1024:(pc + 1) * 1024], itmpd[:, 0:1024],
                        wstat[:, base + 1024:base + 2048], ALU.mult), "vv")
                deq_done = sexp_prev_use[(NM - 1) % 2]
                W("tensor", "vv", max(deq_done, x1q_done))
                pbank = [0, 1] if mi == 0 else [2, 3]
                evac_hist = evac_mi[mi]
                for j in range(8):
                    for sw in range(NSW):
                        kk = j * NSW + sw
                        bank = pbank[kk % 2]
                        if len(evac_hist) >= 2:
                            W("tensor", "aa", evac_hist[-2])
                        for mc in range(NM):
                            op("tensor", lambda e, j=j, sw=sw, mc=mc, bank=bank, wbuf=wbuf: e.matmul(
                                ps[bank][:, :],
                                wbuf[:, mc * 1024 + j * 128: mc * 1024 + j * 128 + 128],
                                A[:, mc * T + sw * SW: mc * T + (sw + 1) * SW],
                                start=(mc == 0), stop=(mc == NM - 1)),
                                "tv" if mc == NM - 1 else None)
                        mml = C["tv"]
                        if kk == 63:
                            mm_last_read_wb = mml
                        sidx = (2 * mi) + (kk % 2)
                        W("scalar", "tv", mml)
                        if spill_slot_dma[sidx]:
                            W("scalar", "wr", spill_slot_dma[sidx])
                            W("scalar", "vv", spill_slot_red[sidx])
                        ae = op("scalar", lambda e, sidx=sidx, bank=bank: e.activation(
                            cst[:, sidx * 512:(sidx + 1) * 512], ps[bank][:, :],
                            ACTF.Copy, bias=0.0, scale=1.0), "aa")
                        evac_hist.append(ae)
                        W("vector", "aa", ae)
                        n = utile if mi == 0 else (192 + gtile)
                        spill_slot_red[sidx] = op("vector", lambda e, sidx=sidx, n=n: e.reduce_max(
                            colstrip[:, n:n + 1], cst[:, sidx * 512:(sidx + 1) * 512],
                            axis=AXX, apply_absolute_value=True), "vv")
                        W("gpsimd", "vv", spill_slot_red[sidx])
                        r0 = ic * IC + j * 128
                        spill_slot_dma[sidx] = dma("gpsimd", "wr",
                                                   spd[r0:r0 + 128, sw * SW:(sw + 1) * SW],
                                                   cst[:, sidx * 512:(sidx + 1) * 512])
                        if mi == 0:
                            utile += 1
                        else:
                            gtile += 1
        p5_spills_done = C["wr"]

        ckpt('P5')
        def derive2():
            vch(lambda e: e.tensor_scalar(
                scal[0:1, 18:19], scal[0:1, 12:13], 1.0 / QMAX, 1e-12, ALU.mult, ALU.max))
            vch(lambda e: e.tensor_scalar(
                scal[0:1, 19:20], scal[0:1, 13:14], 1.0 / QMAX, 1e-12, ALU.mult, ALU.max))
            vch(lambda e: e.reciprocal(scal2[0:1, 22:24], scal[0:1, 18:20]))
            vch(lambda e: e.tensor_copy(scal[0:1, 22:24], scal2[0:1, 22:24]))
            vch(lambda e: e.tensor_scalar(
                scal[0:1, 20:21], scal[0:1, 19:20], 2.0, None, ALU.mult))
            vch(lambda e: e.tensor_copy(scal[0:1, 24:25], scal[0:1, 19:20]))
            vch(lambda e: e.tensor_copy(scal[0:1, 25:26], scal[0:1, 23:24]))
            return vch(lambda e: e.tensor_copy(scal[0:1, 26:27], scal[0:1, 22:23]))
        sync2_done = scale_sync([(0, 192), (192, 192)], ccs_i[1], ccs_o[1], derive2, [24, 25, 26])

        ckpt('sync2')
        W("sync", "wr")
        g2_slot_dma = [0, 0]
        g2_slot_red = [0, 0]
        gi_dma = [0, 0]
        gi_consumed = [0, 0]
        wd_stage_dma = [p5_spills_done, p5_spills_done]
        c1_tile = 0
        W("scalar", "vv", sync2_done)
        prev_g2_tt = sync2_done
        for ipt in range(NI):
            pp = ipt % 2
            base = pp * 3072
            W("sync", "wr")
            W("sync", "vv", sexp_prev_use[pp])
            for b in range(4):
                dma("sync", LQ[pp], wstat[32 * b:32 * (b + 1), base:base + 1024],
                    dI_d[4 * ipt + b:4 * ipt + b + 1, 0:D].broadcast_to([32, D]))
            for b in range(4):
                dma("sync", LQ[pp], wstat[32 * b:32 * (b + 1), base + 1024:base + 2048],
                    dT_d[4 * ipt + b:4 * ipt + b + 1, 0:D].broadcast_to([32, D]))
            dl = dma("sync", LQ[pp], wstat[:, base + 2048:base + 3072],
                     wdT_h[ipt * 128:(ipt + 1) * 128, 0:D])
            W("vector", LQ[pp], dl)
            if wd_stage_dma[pp]:
                W("vector", "wr", wd_stage_dma[pp])
            op("vector", lambda e, base=base: e.tensor_tensor(
                itmpd[:, 0:1024], wstat[:, base + 2048:base + 3072],
                wstat[:, base:base + 1024], ALU.mult))
            sexp_prev_use[pp] = op("vector", lambda e, base=base, pp=pp: e.tensor_tensor(
                spst[:, pp * 1024:(pp + 1) * 1024], itmpd[:, 0:1024],
                wstat[:, base + 1024:base + 2048], ALU.mult), "vv")
            W("gpsimd", "vv", sexp_prev_use[pp])
            wd_stage_dma[pp] = dma("gpsimd", "wr", sp_wd[:, ipt * 1024:(ipt + 1) * 1024],
                                   spst[:, pp * 1024:(pp + 1) * 1024])

        for ipt in range(NI):
            for cw in range(NCW):
                pq = c1_tile % 2
                if gi_dma[pq]:
                    W("sync", "aa")
                    W("sync", "vv", prev_g2_tt)
                gi_dma[pq] = dma("sync", LQ[pq],
                                 cst[:, 5120 + pq * 1024: 5120 + (pq + 1) * 1024],
                                 sp_g[ipt * 128:(ipt + 1) * 128, cw * CW:(cw + 1) * CW])
                W("scalar", LQ[pq], gi_dma[pq])
                if prev_g2_tt:
                    W("scalar", "vv", prev_g2_tt)
                op("scalar", lambda e, pq=pq: e.activation(
                    itmp[:, 0:1024], cst[:, 5120 + pq * 1024: 5120 + (pq + 1) * 1024],
                    ACTF.Copy, bias=0.0, scale=cols[:, 1:2]))
                op("scalar", lambda e: e.activation(
                    ftmp[:, 0:1024], itmp[:, 0:1024], ACTF.Sigmoid, bias=0.0, scale=cols[:, 0:1]))
                gi_consumed[pq] = op("scalar", lambda e: e.activation(
                    itmp[:, 1024:2048], ftmp[:, 0:1024], ACTF.Copy, bias=0.0,
                    scale=ccol[:, 0:1]), "aa")
                W("vector", "aa", gi_consumed[pq])
                op("vector", lambda e: e.tensor_scalar(
                    cst[:, 2048:3072], itmp[:, 1024:2048], 65535.0, 1.0 / 131072.0, ALU.min, ALU.mult))
                if g2_slot_dma[pq]:
                    W("vector", "wr", g2_slot_dma[pq])
                prev_g2_tt = op("vector", lambda e, pq=pq: e.tensor_tensor(
                    cst[:, 3072 + pq * 1024: 3072 + (pq + 1) * 1024],
                    itmp[:, 0:1024], cst[:, 2048:3072], ALU.mult), "vv")
                g2_slot_red[pq] = op("vector", lambda e, pq=pq, n=c1_tile: e.reduce_max(
                    colstrip[:, n:n + 1], cst[:, 3072 + pq * 1024: 3072 + (pq + 1) * 1024],
                    axis=AXX, apply_absolute_value=True), "vv")
                W("gpsimd", "vv", g2_slot_red[pq])
                g2_slot_dma[pq] = dma("gpsimd", "wr",
                                      sp_g2[ipt * 128:(ipt + 1) * 128, cw * CW:(cw + 1) * CW],
                                      cst[:, 3072 + pq * 1024: 3072 + (pq + 1) * 1024])
                c1_tile += 1

        ckpt('P6')
        def derive3():
            vch(lambda e: e.tensor_tensor(
                scal[0:1, 21:22], scal[0:1, 12:13], scal[0:1, 20:21], ALU.mult))
            vch(lambda e: e.tensor_scalar(
                scal[0:1, 27:28], scal[0:1, 21:22], 1.0 / QMAX, 1e-12, ALU.mult, ALU.max))
            vch(lambda e: e.reciprocal(scal2[0:1, 29:31], scal[0:1, 27:29]))
            vch(lambda e: e.tensor_copy(scal[0:1, 29:30], scal2[0:1, 29:30]))
            vch(lambda e: e.tensor_tensor(
                scal[0:1, 30:31], scal[0:1, 20:21], scal[0:1, 29:30], ALU.mult))
            return vch(lambda e: e.tensor_copy(scal[0:1, 31:32], scal[0:1, 22:23]))
        sync3_done = scale_sync([(0, 96)], ccs_i[2], ccs_o[2], derive3, [30, 31])

        ckpt('sync3')
        W("scalar", "vv", sync3_done)
        W("sync", "wr")
        oo_slot_dma = [C["wr"], C["wr"]]
        oo_slot_red = [0, 0]
        in_consumed = [0, 0]
        prev_oo_tt = 0
        c2_tile = 0
        for ipt in range(NI):
            for cw in range(NCW):
                pq = c2_tile % 2
                if c2_tile >= 2:
                    W("sync", "aa")
                d1 = dma("sync", LQ[pq], cst[:, pq * 1024:(pq + 1) * 1024],
                         sp_g2[ipt * 128:(ipt + 1) * 128, cw * CW:(cw + 1) * CW])
                d2 = dma("sync", LQ[pq], cst[:, 2048 + pq * 1024: 2048 + (pq + 1) * 1024],
                         sp_up[ipt * 128:(ipt + 1) * 128, cw * CW:(cw + 1) * CW])
                W("scalar", LQ[pq], d2)
                if prev_oo_tt:
                    W("scalar", "vv", prev_oo_tt)
                op("scalar", lambda e, pq=pq: e.activation(
                    itmp[:, 0:1024], cst[:, pq * 1024:(pq + 1) * 1024],
                    ACTF.Copy, bias=0.0, scale=cols[:, 0:1]))
                in_consumed[pq] = op("scalar", lambda e, pq=pq: e.activation(
                    itmp[:, 1024:2048], cst[:, 2048 + pq * 1024: 2048 + (pq + 1) * 1024],
                    ACTF.Copy, bias=0.0, scale=cols[:, 1:2]), "aa")
                W("vector", "aa", in_consumed[pq])
                W("vector", "wr", oo_slot_dma[pq])
                prev_oo_tt = op("vector", lambda e, pq=pq: e.scalar_tensor_tensor(
                    cst[:, 4096 + pq * 1024: 4096 + (pq + 1) * 1024],
                    itmp[:, 0:1024], 1.0 / 65536.0, itmp[:, 1024:2048],
                    ALU.mult, ALU.mult), "vv")
                oo_slot_red[pq] = op("vector", lambda e, pq=pq, n=c2_tile: e.reduce_max(
                    colstrip[:, n:n + 1], cst[:, 4096 + pq * 1024: 4096 + (pq + 1) * 1024],
                    axis=AXX, apply_absolute_value=True), "vv")
                W("gpsimd", "vv", oo_slot_red[pq])
                oo_slot_dma[pq] = dma("gpsimd", "wr",
                                      sp_o[ipt * 128:(ipt + 1) * 128, cw * CW:(cw + 1) * CW],
                                      cst[:, 4096 + pq * 1024: 4096 + (pq + 1) * 1024])
                c2_tile += 1
                if c2_tile == 3:
                    ckpt("P7a")
                if c2_tile == 48:
                    ckpt("P7b")

        ckpt('P7')
        def derive4():
            vch(lambda e: e.scalar_tensor_tensor(
                scal[0:1, 21:22], scal[0:1, 27:28], 65536.0, scal[0:1, 18:19],
                ALU.mult, ALU.mult))
            vch(lambda e: e.tensor_tensor(
                scal[0:1, 13:14], scal[0:1, 12:13], scal[0:1, 21:22], ALU.mult))
            vch(lambda e: e.tensor_scalar(
                scal[0:1, 15:16], scal[0:1, 13:14], 1.0 / QMAX, 1e-12, ALU.mult, ALU.max))
            vch(lambda e: e.reciprocal(scal2[0:1, 24:26], scal[0:1, 15:17]))
            vch(lambda e: e.tensor_copy(scal[0:1, 24:25], scal2[0:1, 24:25]))
            return vch(lambda e: e.tensor_tensor(
                scal[0:1, 14:15], scal[0:1, 21:22], scal[0:1, 24:25], ALU.mult))
        sync4_done = scale_sync([(0, 96)], ccs_i[3], ccs_o[3], derive4, [14, 15])

        ckpt('sync4')
        W("sync", "wr")
        W("sync", "aa")
        o_in_dma = [0, 0]
        prev_ev = [0, 0]
        vro_prev = [0, 0]
        ro_consumed = [0, 0]
        wd_rd = [0, 0]
        out_dma = 0
        hbank = ps[:7] + [psT]
        stats2_read = sync4_done
        mst2_h = {}
        for sw in range(NSW):
            for ipt in range(NI):
                pq = ipt % 2
                if wd_rd[pq]:
                    W("sync", "tv", ro_consumed[pq])
                wd_rd[pq] = dma("sync", LQ[pq], wdst[:, pq * 1024:(pq + 1) * 1024],
                                sp_wd[:, ipt * 1024:(ipt + 1) * 1024])
                if vro_prev[pq]:
                    W("sync", "vv")
                d2 = dma("sync", LQ[pq], cst[:, pq * 512:(pq + 1) * 512],
                         sp_o[ipt * 128:(ipt + 1) * 128, sw * SW:(sw + 1) * SW])
                W("vector", LQ[pq], max(wd_rd[pq], d2))
                if ro_consumed[pq]:
                    W("vector", "tv", ro_consumed[pq])
                op("vector", lambda e, pq=pq: e.tensor_scalar(
                    itmp[:, 0:512], cst[:, pq * 512:(pq + 1) * 512], cols[:, 0:1], None, ALU.mult))
                vro_prev[pq] = op("vector", lambda e, pq=pq: e.tensor_scalar(
                    rost[:, pq * 512:(pq + 1) * 512], itmp[:, 0:512], 1.0, None, ALU.mult), "vv")
                W("tensor", "vv", vro_prev[pq])
                if sw > 0 and ipt == 0:
                    W("tensor", "vv", stats2_read)
                    W("tensor", "tv", mst2_h[7])
                for h in range(8):
                    ro_consumed[pq] = op("tensor", lambda e, h=h, ipt=ipt, pq=pq: e.matmul(
                        hbank[h][:, :],
                        wdst[:, pq * 1024 + h * 128: pq * 1024 + h * 128 + 128],
                        rost[:, pq * 512:(pq + 1) * 512],
                        start=(ipt == 0), stop=(ipt == NI - 1), skip_group_check=True),
                        "tv" if h == 7 else None)
            dmm = ro_consumed[(NI - 1) % 2]
            W("vector", "tv", dmm)
            ev = {}
            for h in range(8):
                xs1 = 2048 + (h % 2) * 512
                if h >= 2:
                    W("sync", "vv", ev[h - 2])
                elif prev_ev[h]:
                    W("sync", "vv", prev_ev[h])
                x1ld = dma("sync", "lm", wstat[:, xs1:xs1 + 512],
                           sp_x1[h * 128:(h + 1) * 128, sw * SW:(sw + 1) * SW])
                if h >= 2:
                    W("vector", "tv", mst2_h[h - 2])
                W("vector", "lm", x1ld)
                op("vector", lambda e, h=h, xs1=xs1: e.scalar_tensor_tensor(
                    obuf[:, h * 512:(h + 1) * 512], hbank[h][:, :], cols[:, 1:2],
                    wstat[:, xs1:xs1 + 512], ALU.mult, ALU.add))
                if h == 0:
                    op("vector", lambda e, xs1=xs1: e.scalar_tensor_tensor(
                        x1st[:, 512:1024], hbank[0][:, :], cols[:, 1:2],
                        wstat[:, xs1:xs1 + 512], ALU.mult, ALU.add))
                if h == 1:
                    op("vector", lambda e, xs1=xs1: e.scalar_tensor_tensor(
                        ftmp[:, 512:1024], hbank[1][:, :], cols[:, 1:2],
                        wstat[:, xs1:xs1 + 512], ALU.mult, ALU.add))
                sqo = (h % 2) * 512
                ev[h] = op("vector", lambda e, h=h, sqo=sqo: e.tensor_tensor(
                    sqt[:, sqo:sqo + 512], obuf[:, h * 512:(h + 1) * 512],
                    obuf[:, h * 512:(h + 1) * 512], ALU.mult), "vv")
                if h >= 1:
                    hh = h - 1
                    W("tensor", "vv", ev[h])
                    op("tensor", lambda e, hh=hh: e.matmul(
                        ps[0][:, :], ones_b[:, :], obuf[:, hh * 512:(hh + 1) * 512],
                        start=(hh == 0), stop=(hh == 7), skip_group_check=True))
                    mst2_h[hh] = op("tensor", lambda e, hh=hh: e.matmul(
                        ps[1][:, :], ones_b[:, :], sqt[:, (hh % 2) * 512:(hh % 2) * 512 + 512],
                        start=(hh == 0), stop=(hh == 7), skip_group_check=True), "tv")
            W("tensor", "vv", ev[7])
            op("tensor", lambda e: e.matmul(
                ps[0][:, :], ones_b[:, :], obuf[:, 7 * 512:8 * 512],
                start=False, stop=True, skip_group_check=True))
            mst2_h[7] = op("tensor", lambda e: e.matmul(
                ps[1][:, :], ones_b[:, :], sqt[:, 512:1024],
                start=False, stop=True, skip_group_check=True), "tv")
            W("vector", "tv", mst2_h[7])
            op("vector", lambda e: e.tensor_scalar(mtile[:, :], ps[0][:, :], 1.0 / 1024.0, None, ALU.mult))
            op("vector", lambda e: e.tensor_tensor(ttile[:, :], mtile[:, :], mtile[:, :], ALU.mult))
            op("vector", lambda e: e.tensor_scalar(
                rtile[:, :], ps[1][:, :], 1.0 / 1024.0, EPS, ALU.mult, ALU.add))
            va = op("vector", lambda e: e.tensor_tensor(
                ttile[:, :], rtile[:, :], ttile[:, :], ALU.subtract), "vv")
            stats2_read = va
            W("scalar", "vv", va)
            sqv = op("scalar", lambda e: e.activation(rtile[:, :], ttile[:, :], ACTF.Sqrt,
                                                      bias=0.0, scale=1.0), "aa")
            W("vector", "aa", sqv)
            op("vector", lambda e: e.reciprocal(ttile[:, :], rtile[:, :]))
            op("vector", lambda e: e.tensor_tensor(mrtile[:, :], mtile[:, :], ttile[:, :], ALU.mult))
            ap_last = 0
            for h in range(8):
                if h == 0:
                    op("vector", lambda e: e.tensor_tensor(
                        ftmp[:, 0:512], x1st[:, 512:1024], ttile[:, :], ALU.mult))
                elif h == 1:
                    op("vector", lambda e: e.tensor_tensor(
                        ftmp[:, 0:512], ftmp[:, 512:1024], ttile[:, :], ALU.mult))
                else:
                    xs2 = 4096 + (h % 2) * 512
                    if ap_last:
                        W("sync", "vv", ap_last)
                    x2ld = dma("sync", "lm", wstat[:, xs2:xs2 + 512],
                               sp_x1[h * 128:(h + 1) * 128, sw * SW:(sw + 1) * SW])
                    W("vector", "lm", x2ld)
                    op("vector", lambda e, h=h, xs2=xs2: e.scalar_tensor_tensor(
                        ftmp[:, 512:1024], hbank[h][:, :], cols[:, 1:2],
                        wstat[:, xs2:xs2 + 512], ALU.mult, ALU.add))
                    op("vector", lambda e: e.tensor_tensor(
                        ftmp[:, 0:512], ftmp[:, 512:1024], ttile[:, :], ALU.mult))
                if out_dma:
                    W("vector", "wr", out_dma)
                op("vector", lambda e: e.tensor_tensor(
                    x1st[:, 0:512], ftmp[:, 0:512], mrtile[:, :], ALU.subtract))
                vap = op("vector", lambda e: e.reduce_max(
                    rmcol[:, 7:8], x1st[:, 0:512], axis=AXX), "vv")
                ap_last = vap
                W("gpsimd", "vv", vap)
                out_dma = dma("gpsimd", "wr",
                              y_h[h * 128:(h + 1) * 128, sw * SW:(sw + 1) * SW],
                              x1st[:, 0:512])
            stats2_read = ap_last
            prev_ev = [ev[6], ev[7]]
        W("gpsimd", "wr")
        if debug == "lite":
            dma("gpsimd", "wr", dbg_x1[:, :], sp_x1[:, :])
            dma("gpsimd", "wr", dbg_a[:, :], A[:, :])
            dma("gpsimd", "wr", dbg_up[:, :], sp_up[:, :])
            dma("gpsimd", "wr", dbg_g2[:, :], sp_g2[:, :])
            dma("gpsimd", "wr", dbg_o[:, :], sp_o[:, :])
            W("gpsimd", "wr")
        if debug is True:
            dma("gpsimd", "wr", dbg_x1[:, :], sp_x1[:, :])
            dma("gpsimd", "wr", dbg_up[:, :], sp_up[:, :])
            dma("gpsimd", "wr", dbg_g[:, :], sp_g[:, :])
            dma("gpsimd", "wr", dbg_g2[:, :], sp_g2[:, :])
            dma("gpsimd", "wr", dbg_o[:, :], sp_o[:, :])
            dma("gpsimd", "wr", dbg_wd[:, :], sp_wd[:, :])
            dma("gpsimd", "wr", dbg_scal[:, :], scal[:, :])
            dma("gpsimd", "wr", dbg_cols[:, :], cols[:, :])
            dma("gpsimd", "wr", dbg_m[:, :], wb2[:, :])
            dma("gpsimd", "wr", dbg_tu[:, :], dT_up[:, :])
            dma("gpsimd", "wr", dbg_iu[:, :], dI_up[:, :])
            dma("gpsimd", "wr", dbg_tg[:, :], dT_g[:, :])
            dma("gpsimd", "wr", dbg_a[:, :], A[:, :])
            dma("gpsimd", "wr", dbg_vk[:, :], vk_b[:, :])
            W("gpsimd", "wr")

    except _Stop:
        pass

    with nc.Block() as block:
        @block.sync
        def _(eng):
            for f in streams["sync"]:
                f(eng)

        @block.vector
        def _(eng):
            for f in streams["vector"]:
                f(eng)

        @block.scalar
        def _(eng):
            for f in streams["scalar"]:
                f(eng)

        @block.tensor
        def _(eng):
            for f in streams["tensor"]:
                f(eng)

        @block.gpsimd
        def _(eng):
            for f in streams["gpsimd"]:
                f(eng)

    es.close()
    return nc


_NC_CACHE = None


def kernel(L0_attn_x, L0_attn_W_q, L0_attn_W_k, L0_attn_W_v, L0_attn_W_o,
           L0_ffn_w_gate, L0_ffn_w_up, L0_ffn_w_down):
    global _NC_CACHE
    x = np.asarray(L0_attn_x, dtype=np.float32)
    W_q = np.asarray(L0_attn_W_q, dtype=np.float32)
    W_k = np.asarray(L0_attn_W_k, dtype=np.float32)
    W_v = np.asarray(L0_attn_W_v, dtype=np.float32)
    W_o = np.asarray(L0_attn_W_o, dtype=np.float32)
    w_g = np.asarray(L0_ffn_w_gate, dtype=np.float32)
    w_u = np.asarray(L0_ffn_w_up, dtype=np.float32)
    w_d = np.asarray(L0_ffn_w_down, dtype=np.float32)

    wqT = np.ascontiguousarray(W_q.T)
    wkT = np.ascontiguousarray(W_k.T)
    wvT = np.ascontiguousarray(W_v.T)
    wo_pk = np.ascontiguousarray(
        W_o.T.reshape(16, 64, 1024).transpose(1, 0, 2).reshape(64, 16 * 1024))
    wupT = np.ascontiguousarray(w_u.T)
    wgT = np.ascontiguousarray(w_g.T)
    wdT = np.ascontiguousarray(w_d.T)
    ones_b = np.ones((128, 128), dtype=BF)
    ident = np.eye(128, dtype=np.float32)
    ones_row = np.ones((1, 128), dtype=np.float32)

    if _NC_CACHE is None:
        _NC_CACHE = build_nc()
    nc = _NC_CACHE

    in_maps = []
    for c in range(8):
        b, half = c // 2, c % 2
        x0T = np.ascontiguousarray(x[b, half * T:(half + 1) * T, :].T)
        in_maps.append({
            "x0T": x0T, "wqT": wqT, "wkT": wkT, "wvT": wvT, "wo_pk": wo_pk,
            "wupT": wupT, "wgT": wgT, "wdT": wdT,
            "wup_n": np.ascontiguousarray(w_u), "wg_n": np.ascontiguousarray(w_g),
            "wd_n": np.ascontiguousarray(w_d),
            "ones_b": ones_b, "ident": ident, "ones_row": ones_row,
        })

    res = run_bass_kernel_spmd(nc, in_maps, core_ids=list(range(8)))
    out = np.empty((4, 8192, 1024), dtype=np.float32)
    for c in range(8):
        b, half = c // 2, c % 2
        out[b, half * T:(half + 1) * T, :] = res.results[c]["y"].T
    return out
